# version 14
# baseline (speedup 1.0000x reference)
import numpy as np
import ml_dtypes
import concourse.bass as bass
import concourse.mybir as mybir
from concourse.bass_utils import run_bass_kernel_spmd

F32 = mybir.dt.float32
BF16 = mybir.dt.bfloat16
AF = mybir.ActivationFunctionType
ALU = mybir.AluOpType


class CFG:
    S = 2048
    D = 4096
    QR = 1536
    KVR = 512
    RD = 64
    ND = 128
    VD = 128
    NH = 32
    INTER = 11008
    TG = 512
    QG = 512
    SHIFT = 6.0
    EPS_RMS = 1e-6
    EPS_LN = 1e-5


class Buf:
    __slots__ = ("name", "wr", "readers")

    def __init__(self, name):
        self.name = name
        self.wr = None
        self.readers = []


class Op:
    __slots__ = ("eng", "kind", "fn", "args", "kwargs", "sem", "val", "demand")

    def __init__(self, eng, kind, fn=None, args=(), kwargs=None, sem=None, val=0):
        self.eng = eng
        self.kind = kind
        self.fn = fn
        self.args = args
        self.kwargs = kwargs or {}
        self.sem = sem
        self.val = val
        self.demand = False


class Plan:
    def __init__(self):
        self.ops = []
        self.dma_cnt = {}
        self.active_dma = set()
        self.last_call = {}

    def _wait_target(self, eng, src):
        if src is None:
            return None
        if src.kind == "dma":
            return ("dma", src.sem, src.val)
        if src.eng == eng and eng == "T":
            return None
        src.demand = True
        return ("eng", src.eng, src)

    def _emit_waits(self, eng, bufs_read, bufs_written):
        targets = []
        for b in bufs_read:
            t = self._wait_target(eng, b.wr)
            if t is not None:
                targets.append(t)
        for b in bufs_written:
            t = self._wait_target(eng, b.wr)
            if t is not None:
                targets.append(t)
            for r in b.readers:
                t = self._wait_target(eng, r)
                if t is not None:
                    targets.append(t)
        for t in targets:
            self.ops.append(Op(eng, "wait", args=(t,)))

    def call(self, eng, fn, *args, reads=(), writes=(), **kwargs):
        self._emit_waits(eng, reads, writes)
        op = Op(eng, "call", fn=fn, args=args, kwargs=kwargs)
        self.ops.append(op)
        self.last_call[eng] = op
        for b in reads:
            if op not in b.readers:
                b.readers.append(op)
        for b in writes:
            b.wr = op
            b.readers = []
        return op

    def dma(self, q, out_ap, in_ap, sem, reads=(), writes=(), **kwargs):
        self._emit_waits(q, reads, writes)
        cnt = self.dma_cnt.get(sem, 0) + 16
        self.dma_cnt[sem] = cnt
        self.active_dma.add(sem)
        op = Op(q, "dma", args=(out_ap, in_ap), kwargs=kwargs, sem=sem, val=cnt)
        self.ops.append(op)
        for b in reads:
            if op not in b.readers:
                b.readers.append(op)
        for b in writes:
            b.wr = op
            b.readers = []
        return op

    def barrier(self):
        targets = []
        for e in ("T", "A", "V"):
            op = self.last_call.get(e)
            if op is not None:
                op.demand = True
                targets.append(("eng", e, op))
        for sem in sorted(self.active_dma):
            if sem.startswith("s_c"):
                continue
            targets.append(("dma", sem, self.dma_cnt[sem]))
        self.active_dma = {sem for sem in self.active_dma if sem.startswith("s_c")}
        for e in ("T", "A", "V", "S", "G"):
            for t in targets:
                if t[0] == "eng" and t[1] == e and e == "T":
                    continue
                self.ops.append(Op(e, "wait", args=(t,)))

    def finish(self, final_sems):
        for sem in final_sems:
            self.ops.append(Op("S", "wait", args=(("dma", sem, self.dma_cnt.get(sem, 0)),)))

    def number(self):
        cnt = {"T": 0, "A": 0, "V": 0, "S": 0, "G": 0}
        for op in self.ops:
            if op.kind == "call" and op.demand:
                cnt[op.eng] += 1
                op.val = cnt[op.eng]
        wm = {e: {} for e in ("T", "A", "V", "S", "G")}
        out = []
        for op in self.ops:
            if op.kind == "wait":
                t = op.args[0]
                if t[0] == "dma":
                    sem, val = t[1], t[2]
                else:
                    src = t[2]
                    if not src.demand or src.val == 0:
                        continue
                    sem, val = "ENG_" + t[1], src.val
                if wm[op.eng].get(sem, 0) >= val:
                    continue
                wm[op.eng][sem] = val
                op.args = ((sem, val),)
                out.append(op)
            else:
                out.append(op)
        self.ops = out


class Arena:
    def __init__(self, ap, cols):
        self.ap = ap
        self.cols = cols
        self.off = 0
        self.peak = 0

    def mark(self):
        return self.off

    def reset(self, m):
        self.off = m

    def _take(self, cols):
        cols = (cols + 31) & ~31
        off = self.off
        self.off += cols
        self.peak = max(self.peak, self.off)
        assert self.off <= self.cols, f"SBUF arena overflow: {self.off} > {self.cols}"
        return off

    def bf(self, cols):
        off = self._take(cols)
        return self.ap[:, off:off + cols]

    def f32(self, elems):
        off = self._take(2 * elems)
        return self.ap[:, off:off + 2 * elems].bitcast(F32)

def build(cfg):
    S, D, QR, KVR, RD, ND, VD, NH, INTER = (cfg.S, cfg.D, cfg.QR, cfg.KVR, cfg.RD,
                                            cfg.ND, cfg.VD, cfg.NH, cfg.INTER)
    TG, QG = cfg.TG, cfg.QG
    DK, QKc, KKc, IC = D // 128, QR // 128, KVR // 128, INTER // 128
    OD = NH * VD
    ODK = OD // 128
    TCH = S // 128
    NT, NQ = S // TG, S // QG
    QKD = ND + RD
    HD = ND + VD
    assert ND == 128 and VD == 128 and RD == 64

    nc = bass.Bass("TRN2", target_bir_lowering=False)

    xT = nc.dram_tensor("xT", [D, S], F32, kind="ExternalInput").ap()
    w_qa = nc.dram_tensor("w_qa", [D, QR], F32, kind="ExternalInput").ap()
    w_qb_r = nc.dram_tensor("w_qb_r", [NH, 128, QKc, QKD], F32, kind="ExternalInput").ap()
    w_kva = nc.dram_tensor("w_kva", [D, KVR + RD], F32, kind="ExternalInput").ap()
    w_kvb_r = nc.dram_tensor("w_kvb_r", [NH, 128, KKc, HD], F32, kind="ExternalInput").ap()
    w_o = nc.dram_tensor("w_o", [OD, D], F32, kind="ExternalInput").ap()
    gateT = nc.dram_tensor("gateT", [D, INTER], F32, kind="ExternalInput").ap()
    upT = nc.dram_tensor("upT", [D, INTER], F32, kind="ExternalInput").ap()
    downT = nc.dram_tensor("downT", [INTER, D], F32, kind="ExternalInput").ap()
    tabs = nc.dram_tensor("tabs", [4, RD, S], F32, kind="ExternalInput").ap()
    g_qa_r = nc.dram_tensor("g_qa_r", [128, QKc], F32, kind="ExternalInput").ap()
    g_kva_r = nc.dram_tensor("g_kva_r", [128, KKc], F32, kind="ExternalInput").ap()

    xbf_d = nc.dram_tensor("xbf_d", [D, S], BF16, kind="Internal").ap()
    wqa_bf = nc.dram_tensor("wqa_bf", [D, QR], BF16, kind="Internal").ap()
    wkva_bf = nc.dram_tensor("wkva_bf", [D, KVR + RD], BF16, kind="Internal").ap()
    wqb_bf = nc.dram_tensor("wqb_bf", [NH, 128, QKc, QKD], BF16, kind="Internal").ap()
    wkvb_bf = nc.dram_tensor("wkvb_bf", [NH, 128, KKc, HD], BF16, kind="Internal").ap()
    wo_bf = nc.dram_tensor("wo_bf", [OD, D], BF16, kind="Internal").ap()
    gate_bf = nc.dram_tensor("gate_bf", [D, INTER], BF16, kind="Internal").ap()
    up_bf = nc.dram_tensor("up_bf", [D, INTER], BF16, kind="Internal").ap()
    down_bf = nc.dram_tensor("down_bf", [INTER, D], BF16, kind="Internal").ap()
    o_dram = nc.dram_tensor("o_dram", [OD, S], BF16, kind="Internal").ap()
    h1n_dram = nc.dram_tensor("h1n_dram", [D, S], F32, kind="Internal").ap()
    h2_dram = nc.dram_tensor("h2_dram", [D, S], F32, kind="Internal").ap()
    outT = nc.dram_tensor("outT", [D, S], F32, kind="ExternalOutput").ap()

    ps = [nc.alloc_psum_tensor(f"ps{i}", [128, 512], F32).ap() for i in range(8)]
    psb = [Buf(f"ps{i}") for i in range(8)]

    ACOLS = 106400
    arena = Arena(nc.alloc_sbuf_tensor("arena", [128, ACOLS], BF16).ap(), ACOLS)

    t_cos = arena.f32(S)
    t_sinr = arena.f32(S)
    t_cosq = arena.f32(S)
    t_sinrq = arena.f32(S)
    g_qa_sb = arena.f32(QKc)
    g_kva_sb = arena.f32(KKc)
    ones_bf = arena.bf(32)
    ones_f32 = arena.f32(16)
    c_shift = arena.f32(16)
    c_qs = arena.f32(16)
    onesrow = arena.f32(160)
    smalls = arena.f32(10 * TG)

    b_tabs = Buf("tabs")
    b_gqa = Buf("gqa")
    b_gkva = Buf("gkva")
    b_ones = Buf("ones")

    class Rows:
        def __init__(self, region, tg):
            self.region = region
            self.tg = tg
            self.n = 0

        def row(self):
            r = self.region[0:1, self.n * self.tg:(self.n + 1) * self.tg]
            self.n += 1
            assert self.n <= 10
            return r

    m_persist = arena.mark()
    qcn = arena.bf(QKc * S)
    kcn = arena.bf(KKc * S)
    krope = arena.bf(S)
    b_qcn = Buf("qcn")
    b_kcn = Buf("kcn")
    b_krope = Buf("krope")

    p = Plan()

    def onesrow_for(row_ap):
        b = row_ap.base_partition()
        return onesrow[b:b + 1, 0:128]

    for i, t in enumerate((t_cos, t_sinr, t_cosq, t_sinrq)):
        p.dma("S", t[0:64, :], tabs[i], "s_tab", writes=(b_tabs,))
    p.dma("S", g_qa_sb, g_qa_r, "s_gqa", writes=(b_gqa,))
    p.dma("S", g_kva_sb, g_kva_r, "s_gkva", writes=(b_gkva,))
    p.call("V", "memset", ones_bf[:, 0:1], 1.0, writes=(b_ones,))
    p.call("V", "memset", ones_f32[:, 0:1], 1.0, writes=(b_ones,))
    p.call("V", "memset", onesrow[:, 0:128], 1.0, writes=(b_ones,))
    p.call("V", "memset", c_shift[:, 0:1], -float(cfg.SHIFT), writes=(b_ones,))
    p.call("V", "memset", c_qs[:, 0:1], float(ND + RD) ** -0.5, writes=(b_ones,))

    cast_list = []
    cast_list.append((xbf_d, xT, D, S, 512, "s_cx"))
    cast_list.append((wqa_bf, w_qa, D, QR, 1024, "s_cwqa"))
    cast_list.append((wkva_bf, w_kva, D, KVR + RD, 2048, "s_cwkva"))
    cast_list.append((wqb_bf.rearrange("h p k d -> (h p) (k d)"),
                      w_qb_r.rearrange("h p k d -> (h p) (k d)"),
                      NH * 128, QKc * QKD, 512, "s_cwqb"))
    cast_list.append((wkvb_bf.rearrange("h p k d -> (h p) (k d)"),
                      w_kvb_r.rearrange("h p k d -> (h p) (k d)"),
                      NH * 128, KKc * HD, 1024, "s_cwkvb"))
    cast_totals = {}
    b_cast = {}
    for dst, src, rows, cols, rc, sem in cast_list:
        b_cast[sem] = Buf(sem)
        for r0 in range(0, rows, rc):
            r1 = min(rows, r0 + rc)
            p.dma("G", dst[r0:r1, :], src[r0:r1, :], sem, writes=(b_cast[sem],))
            cast_totals[sem] = cast_totals.get(sem, 0) + 16

    def cast_wait(q, sem):
        p.ops.append(Op(q, "wait", args=(("dma", sem, cast_totals[sem]),)))

    m_p1 = arena.mark()
    x_bf = arena.bf(DK * TG)
    qc_f32 = arena.f32(QKc * TG)
    kc_f32 = arena.f32(KKc * TG)
    kr_raw = arena.f32(TG)
    rot_buf = arena.f32(TG)
    wqa_slab = [arena.bf(512) for _ in range(6)]
    wkva_slab = [arena.bf(512) for _ in range(6)]
    wkr_slab = [arena.bf(64) for _ in range(4)]
    sq_sl = [arena.bf(TG) for _ in range(2)]
    rbc = arena.f32(TG)
    tmpf = arena.f32(TG)
    tmp64a = tmpf
    rows1 = Rows(smalls, TG)
    r_ms = rows1.row(); r_s0 = rows1.row(); r_r0 = rows1.row()
    r_u = rows1.row(); r_s1 = rows1.row(); r_d = rows1.row(); r_inv = rows1.row()

    b_x = Buf("x_bf")
    b_qcf = Buf("qc_f32")
    b_kcf = Buf("kc_f32")
    b_krr = Buf("kr_raw")
    b_rot = Buf("rot_buf")
    b_t64 = Buf("tmp64a")
    b_wqa = [Buf(f"wqa{i}") for i in range(6)]
    b_wkva = [Buf(f"wkva{i}") for i in range(6)]
    b_wkr = [Buf(f"wkr{i}") for i in range(4)]
    b_sq = [Buf(f"sq{i}") for i in range(2)]
    b_rbc = Buf("rbc")
    b_tmpf = Buf("tmpf")
    b_rows = Buf("rows1")

    def rms_chain(n_feat, eps):
        p.call("V", "tensor_scalar_mul", r_ms[:, 0:TG], ps[7][0:1, 0:TG], 1.0 / n_feat,
               reads=(psb[7],), writes=(b_rows,))
        p.call("A", "activation", r_s0[:, 0:TG], r_ms[:, 0:TG], AF.Sqrt,
               reads=(b_rows,), writes=(b_rows,))
        p.call("V", "reciprocal", r_r0[:, 0:TG], r_s0[:, 0:TG], reads=(b_rows,), writes=(b_rows,))
        p.call("V", "tensor_mul", r_u[:, 0:TG], r_ms[:, 0:TG], r_r0[:, 0:TG],
               reads=(b_rows,), writes=(b_rows,))
        p.call("V", "tensor_add", r_s1[:, 0:TG], r_u[:, 0:TG], r_s0[:, 0:TG],
               reads=(b_rows,), writes=(b_rows,))
        p.call("V", "tensor_scalar", r_d[:, 0:TG], r_s1[:, 0:TG], 0.5, eps,
               ALU.mult, ALU.add, reads=(b_rows,), writes=(b_rows,))
        p.call("V", "reciprocal", r_inv[:, 0:TG], r_d[:, 0:TG], reads=(b_rows,), writes=(b_rows,))
        p.call("T", "matmul", ps[6][:, 0:TG], onesrow_for(r_inv), r_inv[:, 0:TG],
               start=True, stop=True, reads=(b_rows, b_ones), writes=(psb[6],))
        p.call("V", "tensor_copy", rbc, ps[6][:, 0:TG], reads=(psb[6],), writes=(b_rbc,))

    cast_wait("S", "s_cx")
    cast_wait("S", "s_cwqa")
    cast_wait("S", "s_cwkva")

    def p1_load_x(tg):
        t0 = tg * TG
        for kc in range(DK):
            p.dma("S", x_bf[:, kc * TG:(kc + 1) * TG], xbf_d[kc * 128:(kc + 1) * 128, t0:t0 + TG],
                  "g_x", writes=(b_x,))

    p1_load_x(0)
    for tg in range(NT):
        t0 = tg * TG

        nMb = (QKc + 3) // 4
        for Mb in range(nMb):
            ms = Mb * 4
            nm = min(4, QKc - ms)
            par = (Mb % 2) * 4
            for kc in range(DK):
                sl = kc % 6
                p.dma("S", wqa_slab[sl][:, 0:nm * 128],
                      wqa_bf[kc * 128:(kc + 1) * 128, ms * 128:(ms + nm) * 128],
                      f"g_wqa{sl}", writes=(b_wqa[sl],))
                for m in range(nm):
                    p.call("T", "matmul", ps[par + m][:, 0:TG],
                           wqa_slab[sl][:, m * 128:(m + 1) * 128],
                           x_bf[:, kc * TG:(kc + 1) * TG],
                           start=(kc == 0), stop=(kc == DK - 1),
                           reads=(b_wqa[sl], b_x), writes=(psb[par + m],))
            for m in range(nm):
                p.call("V", "tensor_copy", qc_f32[:, (ms + m) * TG:(ms + m + 1) * TG],
                       ps[par + m][:, 0:TG], reads=(psb[par + m],), writes=(b_qcf,))

        qsq = arena
        for mq in range(QKc):
            sl = mq % 2
            p.call("A", "activation", sq_sl[sl], qc_f32[:, mq * TG:(mq + 1) * TG], AF.Square,
                   reads=(b_qcf,), writes=(b_sq[sl],))
            p.call("T", "matmul", ps[7][0:1, 0:TG], ones_bf[:, 0:1], sq_sl[sl],
                   start=(mq == 0), stop=(mq == QKc - 1),
                   reads=(b_sq[sl], b_ones), writes=(psb[7],))

        for kc in range(DK):
            sl = kc % 6
            p.dma("S", wkva_slab[sl][:, 0:KKc * 128],
                  wkva_bf[kc * 128:(kc + 1) * 128, 0:KVR], f"g_wkva{sl}", writes=(b_wkva[sl],))
            for m in range(KKc):
                p.call("T", "matmul", ps[m][:, 0:TG],
                       wkva_slab[sl][:, m * 128:(m + 1) * 128],
                       x_bf[:, kc * TG:(kc + 1) * TG],
                       start=(kc == 0), stop=(kc == DK - 1),
                       reads=(b_wkva[sl], b_x), writes=(psb[m],))
        for m in range(KKc):
            p.call("V", "tensor_copy", kc_f32[:, m * TG:(m + 1) * TG], ps[m][:, 0:TG],
                   reads=(psb[m],), writes=(b_kcf,))

        for kc in range(DK):
            sl = kc % 4
            p.dma("S", wkr_slab[sl], wkva_bf[kc * 128:(kc + 1) * 128, KVR:KVR + RD],
                  f"g_wkr{sl}", writes=(b_wkr[sl],))
            p.call("T", "matmul", ps[5][0:64, 0:TG], wkr_slab[sl][:, 0:64],
                   x_bf[:, kc * TG:(kc + 1) * TG],
                   start=(kc == 0), stop=(kc == DK - 1),
                   reads=(b_wkr[sl], b_x), writes=(psb[5],))

        if tg + 1 < NT:
            p1_load_x(tg + 1)

        rms_chain(QR, cfg.EPS_RMS)
        for mq in range(QKc):
            p.call("V", "tensor_mul", tmpf, qc_f32[:, mq * TG:(mq + 1) * TG], rbc,
                   reads=(b_qcf, b_rbc), writes=(b_tmpf,))
            p.call("V", "tensor_scalar_mul", qcn[:, mq * S + t0:mq * S + t0 + TG],
                   tmpf, g_qa_sb[:, mq:mq + 1],
                   reads=(b_tmpf, b_gqa), writes=(b_qcn,))

        for mq in range(KKc):
            sl = mq % 2
            p.call("A", "activation", sq_sl[sl], kc_f32[:, mq * TG:(mq + 1) * TG], AF.Square,
                   reads=(b_kcf,), writes=(b_sq[sl],))
            p.call("T", "matmul", ps[7][0:1, 0:TG], ones_bf[:, 0:1], sq_sl[sl],
                   start=(mq == 0), stop=(mq == KKc - 1),
                   reads=(b_sq[sl], b_ones), writes=(psb[7],))
        rms_chain(KVR, cfg.EPS_RMS)
        for mq in range(KKc):
            p.call("V", "tensor_mul", tmpf, kc_f32[:, mq * TG:(mq + 1) * TG], rbc,
                   reads=(b_kcf, b_rbc), writes=(b_tmpf,))
            p.call("V", "tensor_scalar_mul", kcn[:, mq * S + t0:mq * S + t0 + TG],
                   tmpf, g_kva_sb[:, mq:mq + 1],
                   reads=(b_tmpf, b_gkva), writes=(b_kcn,))

        p.call("V", "tensor_copy", kr_raw[0:64, :], ps[5][0:64, 0:TG],
               reads=(psb[5],), writes=(b_krr,))
        p.dma("S", rot_buf[0:32, :], kr_raw[32:64, :], "s_rot", reads=(b_krr,), writes=(b_rot,))
        p.dma("S", rot_buf[32:64, :], kr_raw[0:32, :], "s_rot", reads=(b_krr,), writes=(b_rot,))
        p.call("V", "tensor_mul", tmp64a[0:64, :], kr_raw[0:64, :], t_cos[0:64, t0:t0 + TG],
               reads=(b_krr, b_tabs), writes=(b_t64, b_tmpf))
        p.call("V", "tensor_mul", rot_buf[0:64, :], rot_buf[0:64, :], t_sinr[0:64, t0:t0 + TG],
               reads=(b_rot, b_tabs), writes=(b_rot,))
        p.call("V", "tensor_add", krope[0:64, t0:t0 + TG], tmp64a[0:64, :], rot_buf[0:64, :],
               reads=(b_t64, b_rot), writes=(b_krope,))

    p.barrier()
    arena.reset(m_p1)

    k_h = [arena.bf(S) for _ in range(2)]
    v_h = [arena.bf(S) for _ in range(2)]
    qn_h = [arena.bf(S) for _ in range(2)]
    qr_h = [arena.bf(S) for _ in range(2)]
    P_T = arena.bf(TCH * QG)
    wqb_sl = [arena.bf(QKc * QKD) for _ in range(2)]
    wkvb_sl = [arena.bf(KKc * HD) for _ in range(2)]
    qr_raw = arena.f32(QG)
    qrot = arena.f32(QG)
    qtmp = arena.f32(QG)
    rbc2 = arena.f32(QG)
    o_ev = [arena.bf(QG) for _ in range(2)]
    rows2 = Rows(smalls, TG)
    r2_r = rows2.row()

    b_kh = [Buf(f"k_h{i}") for i in range(2)]
    b_vh = [Buf(f"v_h{i}") for i in range(2)]
    b_qnh = [Buf(f"qn_h{i}") for i in range(2)]
    b_qrh = [Buf(f"qr_h{i}") for i in range(2)]
    b_PT = Buf("P_T")
    b_wqb = [Buf(f"wqb{i}") for i in range(2)]
    b_wkvb = [Buf(f"wkvb{i}") for i in range(2)]
    b_qrr = Buf("qr_raw"); b_qrot = Buf("qrot"); b_qtmp = Buf("qtmp")
    b_rbc2 = Buf("rbc2")
    b_oev = [Buf(f"o_ev{i}") for i in range(2)]
    b_rows2 = Buf("rows2")
    b_odram = Buf("o_dram")

    cast_wait("S", "s_cwqb")
    cast_wait("S", "s_cwkvb")

    def kvq_gen(h):
        sl = h % 2
        pp = [5, 6]
        pc = [0]

        def bank():
            b = pp[pc[0] % 2]
            pc[0] += 1
            return b

        p.dma("S", wqb_sl[sl], wqb_bf[h].rearrange("p k d -> p (k d)"),
              f"g_wqb{sl}", writes=(b_wqb[sl],))
        p.dma("S", wkvb_sl[sl], wkvb_bf[h].rearrange("p k d -> p (k d)"),
              f"g_wkvb{sl}", writes=(b_wkvb[sl],))
        yield
        for tg in range(NT):
            t0 = tg * TG
            bk = bank()
            for m in range(KKc):
                p.call("T", "matmul", ps[bk][:, 0:TG],
                       wkvb_sl[sl][:, m * HD:m * HD + 128],
                       kcn[:, m * S + t0:m * S + t0 + TG],
                       start=(m == 0), stop=(m == KKc - 1),
                       reads=(b_wkvb[sl], b_kcn), writes=(psb[bk],))
            yield
            p.call("V", "tensor_copy", k_h[sl][:, t0:t0 + TG], ps[bk][:, 0:TG],
                   reads=(psb[bk],), writes=(b_kh[sl],))
            yield
        for tq in range(TCH // 4):
            bk = bank()
            for u in range(4):
                tc = tq * 4 + u
                for m in range(KKc):
                    p.call("T", "matmul", ps[bk][:, u * VD:(u + 1) * VD],
                           kcn[:, m * S + tc * 128:m * S + (tc + 1) * 128],
                           wkvb_sl[sl][:, m * HD + 128:(m + 1) * HD],
                           start=(m == 0), stop=(m == KKc - 1),
                           reads=(b_wkvb[sl], b_kcn), writes=(psb[bk],))
                yield
            p.call("V", "tensor_copy", v_h[sl][:, tq * 512:(tq + 1) * 512], ps[bk][:, 0:512],
                   reads=(psb[bk],), writes=(b_vh[sl],))
            yield
        for tg in range(NT):
            t0 = tg * TG
            bk = bank()
            for m in range(QKc):
                p.call("T", "matmul", ps[bk][:, 0:TG],
                       wqb_sl[sl][:, m * QKD:m * QKD + ND],
                       qcn[:, m * S + t0:m * S + t0 + TG],
                       start=(m == 0), stop=(m == QKc - 1),
                       reads=(b_wqb[sl], b_qcn), writes=(psb[bk],))
                if m % 4 == 3:
                    yield
            p.call("A", "activation", qn_h[sl][:, t0:t0 + TG], ps[bk][:, 0:TG], AF.Copy,
                   scale=c_qs[:, 0:1], reads=(psb[bk], b_ones), writes=(b_qnh[sl],))
            yield
            bk = bank()
            for m in range(QKc):
                p.call("T", "matmul", ps[bk][0:64, 0:TG],
                       wqb_sl[sl][:, m * QKD + ND:(m + 1) * QKD],
                       qcn[:, m * S + t0:m * S + t0 + TG],
                       start=(m == 0), stop=(m == QKc - 1),
                       reads=(b_wqb[sl], b_qcn), writes=(psb[bk],))
                if m % 4 == 3:
                    yield
            p.call("V", "tensor_copy", qr_raw[0:64, :], ps[bk][0:64, 0:TG],
                   reads=(psb[bk],), writes=(b_qrr,))
            p.dma("S", qrot[0:32, :], qr_raw[32:64, :], "s_rot", reads=(b_qrr,), writes=(b_qrot,))
            p.dma("S", qrot[32:64, :], qr_raw[0:32, :], "s_rot", reads=(b_qrr,), writes=(b_qrot,))
            yield
            p.call("V", "tensor_mul", qtmp[0:64, :], qr_raw[0:64, :], t_cosq[0:64, t0:t0 + TG],
                   reads=(b_qrr, b_tabs), writes=(b_qtmp,))
            p.call("V", "tensor_mul", qrot[0:64, :], qrot[0:64, :], t_sinrq[0:64, t0:t0 + TG],
                   reads=(b_qrot, b_tabs), writes=(b_qrot,))
            p.call("V", "tensor_add", qr_h[sl][0:64, t0:t0 + TG], qtmp[0:64, :], qrot[0:64, :],
                   reads=(b_qtmp, b_qrot), writes=(b_qrh[sl],))
            yield

    def drain(gen, n=-1):
        if gen is None:
            return
        try:
            if n < 0:
                while True:
                    next(gen)
            else:
                for _ in range(n):
                    next(gen)
        except StopIteration:
            pass

    drain(kvq_gen(0))
    for h in range(NH):
        sl = h % 2
        gen = kvq_gen(h + 1) if h + 1 < NH else None
        for qg in range(NQ):
            q0 = qg * QG
            for kc in range(TCH):
                bk = 2 + kc % 3
                p.call("T", "matmul", ps[bk][:, 0:QG],
                       k_h[sl][:, kc * 128:(kc + 1) * 128], qn_h[sl][:, q0:q0 + QG],
                       start=True, stop=False, reads=(b_kh[sl], b_qnh[sl]), writes=(psb[bk],))
                p.call("T", "matmul", ps[bk][:, 0:QG],
                       krope[0:64, kc * 128:(kc + 1) * 128], qr_h[sl][0:64, q0:q0 + QG],
                       start=False, stop=True, reads=(b_krope, b_qrh[sl]), writes=(psb[bk],))
                p.call("A", "activation", P_T[:, kc * QG:(kc + 1) * QG], ps[bk][:, 0:QG],
                       AF.Exp, bias=c_shift[:, 0:1], reads=(psb[bk], b_ones), writes=(b_PT,))
                if kc % 4 == 3:
                    drain(gen, 2)
            for kc in range(TCH):
                p.call("T", "matmul", ps[7][0:1, 0:QG], ones_bf[:, 0:1],
                       P_T[:, kc * QG:(kc + 1) * QG],
                       start=(kc == 0), stop=(kc == TCH - 1),
                       reads=(b_PT, b_ones), writes=(psb[7],))
            p.call("V", "reciprocal", r2_r[:, 0:QG], ps[7][0:1, 0:QG],
                   reads=(psb[7],), writes=(b_rows2,))
            p.call("T", "matmul", ps[1][:, 0:QG], onesrow_for(r2_r), r2_r[:, 0:QG],
                   start=True, stop=True, reads=(b_rows2, b_ones), writes=(psb[1],))
            p.call("V", "tensor_copy", rbc2, ps[1][:, 0:QG], reads=(psb[1],), writes=(b_rbc2,))
            for kc in range(TCH):
                p.call("T", "matmul", ps[0][:, 0:QG],
                       v_h[sl][:, kc * 128:(kc + 1) * 128], P_T[:, kc * QG:(kc + 1) * QG],
                       start=(kc == 0), stop=(kc == TCH - 1),
                       reads=(b_vh[sl], b_PT), writes=(psb[0],))
            oev = o_ev[qg % 2]
            p.call("V", "tensor_tensor", oev, ps[0][:, 0:QG], rbc2, op=ALU.mult,
                   reads=(psb[0], b_rbc2), writes=(b_oev[qg % 2],))
            p.dma("S", o_dram[h * 128:(h + 1) * 128, q0:q0 + QG], oev, f"s_o{qg % 2}",
                  reads=(b_oev[qg % 2],), writes=(b_odram,))
            drain(gen, 5)
        drain(gen)

    p.barrier()
    arena.reset(m_persist)


    h1nbf = arena.bf(DK * TG)
    b_h1nbf = Buf("h1nbf")
    m_tg = arena.mark()

    b_h1nd = Buf("h1n_dram")
    b_h2d = Buf("h2_dram")
    b_wobf = Buf("wo_bf")
    b_gbf = Buf("gate_bf")
    b_ubf = Buf("up_bf")
    b_dbf = Buf("down_bf")

    def slab_load(tg, slab_ap, bf_src, f32_src, sem, wb_sem, b_slab, b_scr):
        if tg == 0:
            p.dma("G", slab_ap, f32_src, sem + "G", writes=(b_slab,))
            p.dma("S", bf_src, slab_ap, wb_sem, reads=(b_slab,), writes=(b_scr,))
        else:
            p.dma("S", slab_ap, bf_src, sem, reads=(b_scr,), writes=(b_slab,))

    DMB = DK // 4
    IBS = []
    ic0 = 0
    while ic0 < IC:
        IBS.append((ic0, min(4, IC - ic0)))
        ic0 += 4

    for tg in range(NT):
        t0 = tg * TG

        o_tile = arena.bf(ODK * TG)
        wo_sl = [arena.bf(512) for _ in range(4)]
        h1_f32 = arena.f32(DK * TG)
        x_sl = [arena.f32(TG) for _ in range(2)]
        hb_sl = [arena.bf(TG) for _ in range(2)]
        sq3_sl = [arena.bf(TG) for _ in range(2)]
        h1n_sl = [arena.f32(TG) for _ in range(2)]
        abc = arena.f32(TG)
        bbc = arena.f32(TG)
        tmp3 = x_sl[0]
        rows3 = Rows(smalls, TG)
        r3_mean = rows3.row(); r3_msq = rows3.row()
        r3_var = rows3.row(); r3_s0 = rows3.row(); r3_r0 = rows3.row(); r3_a = rows3.row()
        r3_b = rows3.row(); r3_c = rows3.row(); r3_A = rows3.row(); r3_B = rows3.row()

        b_ot = Buf("o_tile")
        b_wo = [Buf(f"wo{i}") for i in range(4)]
        b_h1 = Buf("h1_f32")
        b_xs = [Buf(f"x{i}") for i in range(2)]
        b_hb = [Buf(f"hb{i}") for i in range(2)]
        b_sq3 = [Buf(f"sq3{i}") for i in range(2)]
        b_h1n = [Buf(f"h1n{i}") for i in range(2)]
        b_abc = Buf("abc"); b_bbc = Buf("bbc")
        b_tmp3 = Buf("tmp3")
        b_rows3 = Buf("rows3")

        for kc in range(ODK):
            p.dma("S", o_tile[:, kc * TG:(kc + 1) * TG],
                  o_dram[kc * 128:(kc + 1) * 128, t0:t0 + TG], "s_ol",
                  reads=(b_odram,), writes=(b_ot,))
        for mcb in range(DMB):
            par = (mcb % 2) * 4
            for m in range(2):
                mc = mcb * 4 + m
                p.dma("S", x_sl[m], xT[mc * 128:(mc + 1) * 128, t0:t0 + TG], f"s_x{m}",
                      writes=(b_xs[m],))
            for kc in range(ODK):
                wsl = kc % 4
                slab_load(tg, wo_sl[wsl],
                          wo_bf[kc * 128:(kc + 1) * 128, mcb * 512:(mcb + 1) * 512],
                          w_o[kc * 128:(kc + 1) * 128, mcb * 512:(mcb + 1) * 512],
                          f"s_wo{wsl}", "s_wbo", b_wo[wsl], b_wobf)
                for m in range(4):
                    p.call("T", "matmul", ps[par + m][:, 0:TG],
                           wo_sl[wsl][:, m * 128:(m + 1) * 128],
                           o_tile[:, kc * TG:(kc + 1) * TG],
                           start=(kc == 0), stop=(kc == ODK - 1),
                           reads=(b_wo[wsl], b_ot), writes=(psb[par + m],))
            for m in range(4):
                mc = mcb * 4 + m
                xs = m % 2
                if m >= 2:
                    p.dma("S", x_sl[xs], xT[mc * 128:(mc + 1) * 128, t0:t0 + TG], f"s_x{xs}",
                          writes=(b_xs[xs],))
                p.call("V", "tensor_add", h1_f32[:, mc * TG:(mc + 1) * TG],
                       ps[par + m][:, 0:TG], x_sl[xs],
                       reads=(psb[par + m], b_xs[xs]), writes=(b_h1,))

        for mc in range(DK):
            sl2 = mc % 2
            p.call("V", "tensor_copy", hb_sl[sl2], h1_f32[:, mc * TG:(mc + 1) * TG],
                   reads=(b_h1,), writes=(b_hb[sl2],))
            p.call("A", "activation", sq3_sl[sl2], h1_f32[:, mc * TG:(mc + 1) * TG], AF.Square,
                   reads=(b_h1,), writes=(b_sq3[sl2],))
            p.call("T", "matmul", ps[0][0:1, 0:TG], ones_bf[:, 0:1], hb_sl[sl2],
                   start=(mc == 0), stop=(mc == DK - 1),
                   reads=(b_hb[sl2], b_ones), writes=(psb[0],))
            p.call("T", "matmul", ps[1][0:1, 0:TG], ones_bf[:, 0:1], sq3_sl[sl2],
                   start=(mc == 0), stop=(mc == DK - 1),
                   reads=(b_sq3[sl2], b_ones), writes=(psb[1],))

        def ln_rows(rr, b_rows_, n):
            p.call("V", "tensor_scalar_mul", rr["mean"][:, 0:TG], ps[0][0:1, 0:TG], 1.0 / n,
                   reads=(psb[0],), writes=(b_rows_,))
            p.call("V", "tensor_scalar_mul", rr["msq"][:, 0:TG], ps[1][0:1, 0:TG], 1.0 / n,
                   reads=(psb[1],), writes=(b_rows_,))
            p.call("V", "tensor_mul", rr["a"][:, 0:TG], rr["mean"][:, 0:TG], rr["mean"][:, 0:TG],
                   reads=(b_rows_,), writes=(b_rows_,))
            p.call("V", "tensor_sub", rr["var"][:, 0:TG], rr["msq"][:, 0:TG], rr["a"][:, 0:TG],
                   reads=(b_rows_,), writes=(b_rows_,))
            p.call("V", "tensor_scalar_add", rr["var"][:, 0:TG], rr["var"][:, 0:TG], cfg.EPS_LN,
                   reads=(b_rows_,), writes=(b_rows_,))
            p.call("A", "activation", rr["s0"][:, 0:TG], rr["var"][:, 0:TG], AF.Sqrt,
                   reads=(b_rows_,), writes=(b_rows_,))
            p.call("V", "reciprocal", rr["r0"][:, 0:TG], rr["s0"][:, 0:TG],
                   reads=(b_rows_,), writes=(b_rows_,))
            p.call("V", "tensor_mul", rr["a"][:, 0:TG], rr["r0"][:, 0:TG], rr["r0"][:, 0:TG],
                   reads=(b_rows_,), writes=(b_rows_,))
            p.call("V", "tensor_mul", rr["b"][:, 0:TG], rr["a"][:, 0:TG], rr["var"][:, 0:TG],
                   reads=(b_rows_,), writes=(b_rows_,))
            p.call("V", "tensor_scalar", rr["c"][:, 0:TG], rr["b"][:, 0:TG], -0.5, 1.5,
                   ALU.mult, ALU.add, reads=(b_rows_,), writes=(b_rows_,))
            p.call("V", "tensor_mul", rr["A"][:, 0:TG], rr["c"][:, 0:TG], rr["r0"][:, 0:TG],
                   reads=(b_rows_,), writes=(b_rows_,))
            p.call("V", "tensor_mul", rr["b"][:, 0:TG], rr["mean"][:, 0:TG], rr["A"][:, 0:TG],
                   reads=(b_rows_,), writes=(b_rows_,))
            p.call("V", "tensor_scalar_mul", rr["B"][:, 0:TG], rr["b"][:, 0:TG], -1.0,
                   reads=(b_rows_,), writes=(b_rows_,))

        rr3 = {"mean": r3_mean, "msq": r3_msq, "var": r3_var, "s0": r3_s0, "r0": r3_r0,
               "a": r3_a, "b": r3_b, "c": r3_c, "A": r3_A, "B": r3_B}
        ln_rows(rr3, b_rows3, D)
        p.call("T", "matmul", ps[2][:, 0:TG], onesrow_for(r3_A), r3_A[:, 0:TG],
               start=True, stop=True, reads=(b_rows3, b_ones), writes=(psb[2],))
        p.call("T", "matmul", ps[3][:, 0:TG], onesrow_for(r3_B), r3_B[:, 0:TG],
               start=True, stop=True, reads=(b_rows3, b_ones), writes=(psb[3],))
        p.call("V", "tensor_copy", abc, ps[2][:, 0:TG], reads=(psb[2],), writes=(b_abc,))
        p.call("V", "tensor_copy", bbc, ps[3][:, 0:TG], reads=(psb[3],), writes=(b_bbc,))
        for mc in range(DK):
            sl2 = mc % 2
            p.call("V", "tensor_mul", tmp3, h1_f32[:, mc * TG:(mc + 1) * TG], abc,
                   reads=(b_h1, b_abc), writes=(b_tmp3, b_xs[0]))
            p.call("V", "tensor_add", h1n_sl[sl2], tmp3, bbc,
                   reads=(b_tmp3, b_bbc), writes=(b_h1n[sl2],))
            p.dma("S", h1n_dram[mc * 128:(mc + 1) * 128, t0:t0 + TG], h1n_sl[sl2], f"s_h1n{sl2}",
                  reads=(b_h1n[sl2],), writes=(b_h1nd,))
            p.call("A", "activation", h1nbf[:, mc * TG:(mc + 1) * TG], h1n_sl[sl2], AF.Copy,
                   reads=(b_h1n[sl2],), writes=(b_h1nbf,))

        p.barrier()
        arena.reset(m_tg)

        hmid = arena.bf(IC * TG)
        g_sl = [arena.bf(512) for _ in range(4)]
        u_sl = [arena.bf(512) for _ in range(4)]
        silu_sl = [arena.f32(TG) for _ in range(2)]
        b_hm = Buf("hmid")
        b_gsl = [Buf(f"g{i}") for i in range(4)]
        b_usl = [Buf(f"u{i}") for i in range(4)]
        b_silu = [Buf(f"silu{i}") for i in range(2)]

        for (i0, nic) in IBS:
            for kc in range(DK):
                wsl = kc % 4
                slab_load(tg, g_sl[wsl][:, 0:nic * 128],
                          gate_bf[kc * 128:(kc + 1) * 128, i0 * 128:(i0 + nic) * 128],
                          gateT[kc * 128:(kc + 1) * 128, i0 * 128:(i0 + nic) * 128],
                          f"s_g{wsl}", "s_wbg", b_gsl[wsl], b_gbf)
                slab_load(tg, u_sl[wsl][:, 0:nic * 128],
                          up_bf[kc * 128:(kc + 1) * 128, i0 * 128:(i0 + nic) * 128],
                          upT[kc * 128:(kc + 1) * 128, i0 * 128:(i0 + nic) * 128],
                          f"s_u{wsl}", "s_wbu", b_usl[wsl], b_ubf)
                for j in range(nic):
                    p.call("T", "matmul", ps[j][:, 0:TG],
                           g_sl[wsl][:, j * 128:(j + 1) * 128],
                           h1nbf[:, kc * TG:(kc + 1) * TG],
                           start=(kc == 0), stop=(kc == DK - 1),
                           reads=(b_gsl[wsl], b_h1nbf), writes=(psb[j],))
                    p.call("T", "matmul", ps[4 + j][:, 0:TG],
                           u_sl[wsl][:, j * 128:(j + 1) * 128],
                           h1nbf[:, kc * TG:(kc + 1) * TG],
                           start=(kc == 0), stop=(kc == DK - 1),
                           reads=(b_usl[wsl], b_h1nbf), writes=(psb[4 + j],))
            for j in range(nic):
                sl2 = j % 2
                p.call("A", "activation", silu_sl[sl2], ps[j][:, 0:TG], AF.Silu,
                       reads=(psb[j],), writes=(b_silu[sl2],))
                p.call("V", "tensor_tensor", hmid[:, (i0 + j) * TG:(i0 + j + 1) * TG],
                       silu_sl[sl2], ps[4 + j][:, 0:TG], op=ALU.mult,
                       reads=(b_silu[sl2], psb[4 + j]), writes=(b_hm,))

        p.barrier()
        arena.reset(m_tg)
        arena.bf(IC * TG)

        d_sl = [arena.bf(512) for _ in range(4)]
        h1r_sl = [arena.f32(TG) for _ in range(2)]
        h2_sl = [arena.f32(TG) for _ in range(2)]
        b_dsl = [Buf(f"d{i}") for i in range(4)]
        b_h1r = [Buf(f"h1r{i}") for i in range(2)]
        b_h2 = [Buf(f"h2{i}") for i in range(2)]

        for mcb in range(DMB):
            par = (mcb % 2) * 4
            for m in range(2):
                mc = mcb * 4 + m
                p.dma("S", h1r_sl[m], h1n_dram[mc * 128:(mc + 1) * 128, t0:t0 + TG],
                      f"s_h1r{m}", reads=(b_h1nd,), writes=(b_h1r[m],))
            for ic in range(IC):
                wsl = ic % 4
                slab_load(tg, d_sl[wsl],
                          down_bf[ic * 128:(ic + 1) * 128, mcb * 512:(mcb + 1) * 512],
                          downT[ic * 128:(ic + 1) * 128, mcb * 512:(mcb + 1) * 512],
                          f"s_d{wsl}", "s_wbd", b_dsl[wsl], b_dbf)
                for m in range(4):
                    p.call("T", "matmul", ps[par + m][:, 0:TG],
                           d_sl[wsl][:, m * 128:(m + 1) * 128],
                           hmid[:, ic * TG:(ic + 1) * TG],
                           start=(ic == 0), stop=(ic == IC - 1),
                           reads=(b_dsl[wsl], b_hm), writes=(psb[par + m],))
            for m in range(4):
                mc = mcb * 4 + m
                sl2 = mc % 2
                if m >= 2:
                    p.dma("S", h1r_sl[sl2], h1n_dram[mc * 128:(mc + 1) * 128, t0:t0 + TG],
                          f"s_h1r{sl2}", reads=(b_h1nd,), writes=(b_h1r[sl2],))
                p.call("V", "tensor_add", h2_sl[sl2], ps[par + m][:, 0:TG], h1r_sl[sl2],
                       reads=(psb[par + m], b_h1r[sl2]), writes=(b_h2[sl2],))
                p.dma("S", h2_dram[mc * 128:(mc + 1) * 128, t0:t0 + TG], h2_sl[sl2],
                      f"s_h2{sl2}", reads=(b_h2[sl2],), writes=(b_h2d,))

        h2r_sl = [arena.f32(TG) for _ in range(2)]
        out_sl = [arena.f32(TG) for _ in range(2)]
        hb2_sl = [arena.bf(TG) for _ in range(2)]
        sq2_sl = [arena.bf(TG) for _ in range(2)]
        abc2 = arena.f32(TG)
        bbc2 = arena.f32(TG)
        tmp4 = h1r_sl[0]
        b_h2r = [Buf(f"h2r{i}") for i in range(2)]
        b_out = [Buf(f"out{i}") for i in range(2)]
        b_hb2 = [Buf(f"hb2{i}") for i in range(2)]
        b_sq2 = [Buf(f"sq2{i}") for i in range(2)]
        b_abc2 = Buf("abc2"); b_bbc2 = Buf("bbc2")
        b_tmp4 = Buf("tmp4")
        rows4 = Rows(smalls, TG)
        rr4 = {k: rows4.row() for k in ("mean", "msq", "var", "s0", "r0", "a", "b", "c", "A", "B")}
        b_rows4 = Buf("rows4")

        for sem in ("s_h20", "s_h21"):
            if sem in p.dma_cnt:
                p.ops.append(Op("S", "wait", args=(("dma", sem, p.dma_cnt[sem]),)))
        for mc in range(DK):
            sl2 = mc % 2
            p.dma("S", h2r_sl[sl2], h2_dram[mc * 128:(mc + 1) * 128, t0:t0 + TG],
                  f"s_h2r{sl2}", reads=(b_h2d,), writes=(b_h2r[sl2],))
            p.call("V", "tensor_copy", hb2_sl[sl2], h2r_sl[sl2],
                   reads=(b_h2r[sl2],), writes=(b_hb2[sl2],))
            p.call("A", "activation", sq2_sl[sl2], h2r_sl[sl2], AF.Square,
                   reads=(b_h2r[sl2],), writes=(b_sq2[sl2],))
            p.call("T", "matmul", ps[0][0:1, 0:TG], ones_bf[:, 0:1], hb2_sl[sl2],
                   start=(mc == 0), stop=(mc == DK - 1),
                   reads=(b_hb2[sl2], b_ones), writes=(psb[0],))
            p.call("T", "matmul", ps[1][0:1, 0:TG], ones_bf[:, 0:1], sq2_sl[sl2],
                   start=(mc == 0), stop=(mc == DK - 1),
                   reads=(b_sq2[sl2], b_ones), writes=(psb[1],))
        ln_rows(rr4, b_rows4, D)
        p.call("T", "matmul", ps[2][:, 0:TG], onesrow_for(rr4["A"]), rr4["A"][:, 0:TG],
               start=True, stop=True, reads=(b_rows4, b_ones), writes=(psb[2],))
        p.call("T", "matmul", ps[3][:, 0:TG], onesrow_for(rr4["B"]), rr4["B"][:, 0:TG],
               start=True, stop=True, reads=(b_rows4, b_ones), writes=(psb[3],))
        p.call("V", "tensor_copy", abc2, ps[2][:, 0:TG], reads=(psb[2],), writes=(b_abc2,))
        p.call("V", "tensor_copy", bbc2, ps[3][:, 0:TG], reads=(psb[3],), writes=(b_bbc2,))
        for mc in range(DK):
            sl2 = mc % 2
            p.dma("S", h2r_sl[sl2], h2_dram[mc * 128:(mc + 1) * 128, t0:t0 + TG],
                  f"s_h2r{sl2}", reads=(b_h2d,), writes=(b_h2r[sl2],))
            p.call("V", "tensor_mul", tmp4, h2r_sl[sl2], abc2,
                   reads=(b_h2r[sl2], b_abc2), writes=(b_tmp4, b_h1r[0]))
            p.call("V", "tensor_add", out_sl[sl2], tmp4, bbc2,
                   reads=(b_tmp4, b_bbc2), writes=(b_out[sl2],))
            p.dma("S", outT[mc * 128:(mc + 1) * 128, t0:t0 + TG], out_sl[sl2], f"s_out{sl2}",
                  reads=(b_out[sl2],))

        p.barrier()
        arena.reset(m_tg)

    p.finish(["s_out0", "s_out1"])
    p.number()

    sem_names = sorted(set([op.sem for op in p.ops if op.kind == "dma"]
                           + [op.args[0][0] for op in p.ops if op.kind == "wait"]
                           + ["ENG_" + op.eng for op in p.ops
                              if op.kind == "call" and op.demand]))
    sem_objs = {name: nc.alloc_semaphore(name) for name in sem_names}

    eng_sem = {e: sem_objs.get("ENG_" + e) for e in ("T", "A", "V", "S", "G")}

    def emit(eng_key, eng):
        for op in p.ops:
            if op.eng != eng_key:
                continue
            if op.kind == "wait":
                sem, val = op.args[0]
                eng.wait_ge(sem_objs[sem], val)
            elif op.kind == "dma":
                instr = eng.dma_start(out=op.args[0], in_=op.args[1], **op.kwargs)
                instr.then_inc(sem_objs[op.sem], 16)
            else:
                instr = getattr(eng, op.fn)(*op.args, **op.kwargs)
                if op.demand:
                    instr.then_inc(eng_sem[op.eng], 1)

    with nc.Block() as block:
        @block.tensor
        def _(t):
            emit("T", t)

        @block.scalar
        def _(a):
            emit("A", a)

        @block.vector
        def _(v):
            emit("V", v)

        @block.sync
        def _(s):
            emit("S", s)

        @block.gpsimd
        def _(g):
            emit("G", g)

    nc.finalize()
    return nc


def host_inputs(cfg, x_b, w_qa, g_qa, w_qb, w_kva, g_kva, w_kvb, w_o, w_gate, w_up, w_down):
    S, RD, NH, QKc, KKc = cfg.S, cfg.RD, cfg.NH, cfg.QR // 128, cfg.KVR // 128
    QKD = cfg.ND + cfg.RD
    HD = cfg.ND + cfg.VD
    scl = float(QKD) ** -0.5

    inv_freq = (1.0 / (10000.0 ** (np.arange(0, RD, 2, dtype=np.float32) / RD))).astype(np.float32)
    t = np.arange(S, dtype=np.float32)
    freqs = np.outer(t, inv_freq).astype(np.float32)
    emb = np.concatenate([freqs, freqs], axis=1)
    cos = np.cos(emb).T.astype(np.float32)
    sin = np.sin(emb).T.astype(np.float32)
    sin_rot = np.concatenate([-sin[:RD // 2], sin[RD // 2:]], axis=0).astype(np.float32)
    tabs = np.stack([cos, sin_rot, (cos * scl), (sin_rot * scl)]).astype(np.float32)

    w_qb_r = np.ascontiguousarray(
        w_qb.reshape(QKc, 128, NH, QKD).transpose(2, 1, 0, 3)).astype(np.float32)
    w_kvb_r = np.ascontiguousarray(
        w_kvb.reshape(KKc, 128, NH, HD).transpose(2, 1, 0, 3)).astype(np.float32)
    g_qa_r = np.ascontiguousarray(g_qa.reshape(QKc, 128).T).astype(np.float32)
    g_kva_r = np.ascontiguousarray(g_kva.reshape(KKc, 128).T).astype(np.float32)

    return {
        "xT": np.ascontiguousarray(x_b.T).astype(np.float32),
        "w_qa": np.ascontiguousarray(w_qa).astype(np.float32),
        "w_qb_r": w_qb_r,
        "w_kva": np.ascontiguousarray(w_kva).astype(np.float32),
        "w_kvb_r": w_kvb_r,
        "w_o": np.ascontiguousarray(w_o).astype(np.float32),
        "gateT": np.ascontiguousarray(w_gate.T).astype(np.float32),
        "upT": np.ascontiguousarray(w_up.T).astype(np.float32),
        "downT": np.ascontiguousarray(w_down.T).astype(np.float32),
        "tabs": tabs,
        "g_qa_r": g_qa_r,
        "g_kva_r": g_kva_r,
    }


def kernel(L0_attn_hidden_states, L0_attn_w_qa, L0_attn_g_qa, L0_attn_w_qb,
           L0_attn_w_kva, L0_attn_g_kva, L0_attn_w_kvb, L0_attn_w_o,
           L0_ffn_w_gate, L0_ffn_w_up, L0_ffn_w_down):
    cfg = CFG()
    x = np.asarray(L0_attn_hidden_states, dtype=np.float32)
    B = x.shape[0]
    assert B == 8 and x.shape[1] == cfg.S and x.shape[2] == cfg.D

    nc = build(cfg)

    shared = None
    in_maps = []
    for b in range(B):
        m = host_inputs(cfg, x[b],
                        np.asarray(L0_attn_w_qa), np.asarray(L0_attn_g_qa),
                        np.asarray(L0_attn_w_qb), np.asarray(L0_attn_w_kva),
                        np.asarray(L0_attn_g_kva), np.asarray(L0_attn_w_kvb),
                        np.asarray(L0_attn_w_o), np.asarray(L0_ffn_w_gate),
                        np.asarray(L0_ffn_w_up), np.asarray(L0_ffn_w_down))
        if shared is None:
            shared = {k: v for k, v in m.items() if k != "xT"}
        else:
            for k in shared:
                m[k] = shared[k]
        in_maps.append(m)

    res = run_bass_kernel_spmd(nc, in_maps, core_ids=list(range(8)))
    out = np.stack([np.asarray(r["outT"]).T for r in res.results], axis=0)
    return np.ascontiguousarray(out.astype(np.float32))


# version 15 (final)
# speedup vs baseline: 1.0891x; 1.0891x over previous
import numpy as np
import ml_dtypes
import concourse.bass as bass
import concourse.mybir as mybir
from concourse.bass_utils import run_bass_kernel_spmd

F32 = mybir.dt.float32
BF16 = mybir.dt.bfloat16
AF = mybir.ActivationFunctionType
ALU = mybir.AluOpType


class CFG:
    S = 2048
    D = 4096
    QR = 1536
    KVR = 512
    RD = 64
    ND = 128
    VD = 128
    NH = 32
    INTER = 11008
    TG = 512
    QG = 512
    SHIFT = 6.0
    EPS_RMS = 1e-6
    EPS_LN = 1e-5


class Buf:
    __slots__ = ("name", "wr", "readers")

    def __init__(self, name):
        self.name = name
        self.wr = None
        self.readers = []


class Op:
    __slots__ = ("eng", "kind", "fn", "args", "kwargs", "sem", "val", "demand")

    def __init__(self, eng, kind, fn=None, args=(), kwargs=None, sem=None, val=0):
        self.eng = eng
        self.kind = kind
        self.fn = fn
        self.args = args
        self.kwargs = kwargs or {}
        self.sem = sem
        self.val = val
        self.demand = False


class Plan:
    def __init__(self):
        self.ops = []
        self.dma_cnt = {}
        self.active_dma = set()
        self.last_call = {}

    def _wait_target(self, eng, src):
        if src is None:
            return None
        if src.kind == "dma":
            return ("dma", src.sem, src.val)
        if src.eng == eng and eng == "T":
            return None
        src.demand = True
        return ("eng", src.eng, src)

    def _emit_waits(self, eng, bufs_read, bufs_written):
        targets = []
        for b in bufs_read:
            t = self._wait_target(eng, b.wr)
            if t is not None:
                targets.append(t)
        for b in bufs_written:
            t = self._wait_target(eng, b.wr)
            if t is not None:
                targets.append(t)
            for r in b.readers:
                t = self._wait_target(eng, r)
                if t is not None:
                    targets.append(t)
        for t in targets:
            self.ops.append(Op(eng, "wait", args=(t,)))

    def call(self, eng, fn, *args, reads=(), writes=(), **kwargs):
        self._emit_waits(eng, reads, writes)
        op = Op(eng, "call", fn=fn, args=args, kwargs=kwargs)
        self.ops.append(op)
        self.last_call[eng] = op
        for b in reads:
            if op not in b.readers:
                b.readers.append(op)
        for b in writes:
            b.wr = op
            b.readers = []
        return op

    def dma(self, q, out_ap, in_ap, sem, reads=(), writes=(), **kwargs):
        self._emit_waits(q, reads, writes)
        cnt = self.dma_cnt.get(sem, 0) + 16
        self.dma_cnt[sem] = cnt
        self.active_dma.add(sem)
        op = Op(q, "dma", args=(out_ap, in_ap), kwargs=kwargs, sem=sem, val=cnt)
        self.ops.append(op)
        for b in reads:
            if op not in b.readers:
                b.readers.append(op)
        for b in writes:
            b.wr = op
            b.readers = []
        return op

    def barrier(self):
        targets = []
        for e in ("T", "A", "V"):
            op = self.last_call.get(e)
            if op is not None:
                op.demand = True
                targets.append(("eng", e, op))
        for sem in sorted(self.active_dma):
            if sem.startswith("s_c"):
                continue
            targets.append(("dma", sem, self.dma_cnt[sem]))
        self.active_dma = {sem for sem in self.active_dma if sem.startswith("s_c")}
        for e in ("T", "A", "V", "S", "G"):
            for t in targets:
                if t[0] == "eng" and t[1] == e and e == "T":
                    continue
                self.ops.append(Op(e, "wait", args=(t,)))

    def finish(self, final_sems):
        for sem in final_sems:
            self.ops.append(Op("S", "wait", args=(("dma", sem, self.dma_cnt.get(sem, 0)),)))

    def number(self):
        cnt = {"T": 0, "A": 0, "V": 0, "S": 0, "G": 0}
        for op in self.ops:
            if op.kind == "call" and op.demand:
                cnt[op.eng] += 1
                op.val = cnt[op.eng]
        wm = {e: {} for e in ("T", "A", "V", "S", "G")}
        out = []
        for op in self.ops:
            if op.kind == "wait":
                t = op.args[0]
                if t[0] == "dma":
                    sem, val = t[1], t[2]
                else:
                    src = t[2]
                    if not src.demand or src.val == 0:
                        continue
                    sem, val = "ENG_" + t[1], src.val
                if wm[op.eng].get(sem, 0) >= val:
                    continue
                wm[op.eng][sem] = val
                op.args = ((sem, val),)
                out.append(op)
            else:
                out.append(op)
        self.ops = out


class Arena:
    def __init__(self, ap, cols):
        self.ap = ap
        self.cols = cols
        self.off = 0
        self.peak = 0

    def mark(self):
        return self.off

    def reset(self, m):
        self.off = m

    def _take(self, cols):
        cols = (cols + 31) & ~31
        off = self.off
        self.off += cols
        self.peak = max(self.peak, self.off)
        assert self.off <= self.cols, f"SBUF arena overflow: {self.off} > {self.cols}"
        return off

    def bf(self, cols):
        off = self._take(cols)
        return self.ap[:, off:off + cols]

    def f32(self, elems):
        off = self._take(2 * elems)
        return self.ap[:, off:off + 2 * elems].bitcast(F32)

def build(cfg):
    S, D, QR, KVR, RD, ND, VD, NH, INTER = (cfg.S, cfg.D, cfg.QR, cfg.KVR, cfg.RD,
                                            cfg.ND, cfg.VD, cfg.NH, cfg.INTER)
    TG, QG = cfg.TG, cfg.QG
    DK, QKc, KKc, IC = D // 128, QR // 128, KVR // 128, INTER // 128
    OD = NH * VD
    ODK = OD // 128
    TCH = S // 128
    NT, NQ = S // TG, S // QG
    QKD = ND + RD
    HD = ND + VD
    assert ND == 128 and VD == 128 and RD == 64

    nc = bass.Bass("TRN2", target_bir_lowering=False)

    xT = nc.dram_tensor("xT", [D, S], F32, kind="ExternalInput").ap()
    w_qa = nc.dram_tensor("w_qa", [D, QR], F32, kind="ExternalInput").ap()
    w_qb_r = nc.dram_tensor("w_qb_r", [NH, 128, QKc, QKD], F32, kind="ExternalInput").ap()
    w_kva = nc.dram_tensor("w_kva", [D, KVR + RD], F32, kind="ExternalInput").ap()
    w_kvb_r = nc.dram_tensor("w_kvb_r", [NH, 128, KKc, HD], F32, kind="ExternalInput").ap()
    w_o = nc.dram_tensor("w_o", [OD, D], F32, kind="ExternalInput").ap()
    gateT = nc.dram_tensor("gateT", [D, INTER], F32, kind="ExternalInput").ap()
    upT = nc.dram_tensor("upT", [D, INTER], F32, kind="ExternalInput").ap()
    downT = nc.dram_tensor("downT", [INTER, D], F32, kind="ExternalInput").ap()
    tabs = nc.dram_tensor("tabs", [4, RD, S], F32, kind="ExternalInput").ap()
    g_qa_r = nc.dram_tensor("g_qa_r", [128, QKc], F32, kind="ExternalInput").ap()
    g_kva_r = nc.dram_tensor("g_kva_r", [128, KKc], F32, kind="ExternalInput").ap()

    xbf_d = nc.dram_tensor("xbf_d", [D, S], BF16, kind="Internal").ap()
    wqa_bf = nc.dram_tensor("wqa_bf", [D, QR], BF16, kind="Internal").ap()
    wkva_bf = nc.dram_tensor("wkva_bf", [D, KVR + RD], BF16, kind="Internal").ap()
    wqb_bf = nc.dram_tensor("wqb_bf", [NH, 128, QKc, QKD], BF16, kind="Internal").ap()
    wkvb_bf = nc.dram_tensor("wkvb_bf", [NH, 128, KKc, HD], BF16, kind="Internal").ap()
    wo_bf = nc.dram_tensor("wo_bf", [OD, D], BF16, kind="Internal").ap()
    gate_bf = nc.dram_tensor("gate_bf", [D, INTER], BF16, kind="Internal").ap()
    up_bf = nc.dram_tensor("up_bf", [D, INTER], BF16, kind="Internal").ap()
    down_bf = nc.dram_tensor("down_bf", [INTER, D], BF16, kind="Internal").ap()
    o_dram = nc.dram_tensor("o_dram", [OD, S], BF16, kind="Internal").ap()
    h1n_dram = nc.dram_tensor("h1n_dram", [D, S], F32, kind="Internal").ap()
    h2_dram = nc.dram_tensor("h2_dram", [D, S], F32, kind="Internal").ap()
    outT = nc.dram_tensor("outT", [D, S], F32, kind="ExternalOutput").ap()

    ps = [nc.alloc_psum_tensor(f"ps{i}", [128, 512], F32).ap() for i in range(8)]
    psb = [Buf(f"ps{i}") for i in range(8)]

    ACOLS = 106400
    arena = Arena(nc.alloc_sbuf_tensor("arena", [128, ACOLS], BF16).ap(), ACOLS)

    t_cos = arena.f32(S)
    t_sinr = arena.f32(S)
    t_cosq = arena.f32(S)
    t_sinrq = arena.f32(S)
    g_qa_sb = arena.f32(QKc)
    g_kva_sb = arena.f32(KKc)
    ones_bf = arena.bf(32)
    ones_f32 = arena.f32(16)
    c_shift = arena.f32(16)
    c_qs = arena.f32(16)
    onesrow = arena.f32(160)
    smalls = arena.f32(10 * TG)

    b_tabs = Buf("tabs")
    b_gqa = Buf("gqa")
    b_gkva = Buf("gkva")
    b_ones = Buf("ones")

    class Rows:
        def __init__(self, region, tg):
            self.region = region
            self.tg = tg
            self.n = 0

        def row(self):
            r = self.region[0:1, self.n * self.tg:(self.n + 1) * self.tg]
            self.n += 1
            assert self.n <= 10
            return r

    m_persist = arena.mark()
    qcn = arena.bf(QKc * S)
    kcn = arena.bf(KKc * S)
    krope = arena.bf(S)
    b_qcn = Buf("qcn")
    b_kcn = Buf("kcn")
    b_krope = Buf("krope")

    p = Plan()

    def onesrow_for(row_ap):
        b = row_ap.base_partition()
        return onesrow[b:b + 1, 0:128]

    for i, t in enumerate((t_cos, t_sinr, t_cosq, t_sinrq)):
        p.dma("S", t[0:64, :], tabs[i], "s_tab", writes=(b_tabs,))
    p.dma("S", g_qa_sb, g_qa_r, "s_gqa", writes=(b_gqa,))
    p.dma("S", g_kva_sb, g_kva_r, "s_gkva", writes=(b_gkva,))
    p.call("V", "memset", ones_bf[:, 0:1], 1.0, writes=(b_ones,))
    p.call("V", "memset", ones_f32[:, 0:1], 1.0, writes=(b_ones,))
    p.call("V", "memset", onesrow[:, 0:128], 1.0, writes=(b_ones,))
    p.call("V", "memset", c_shift[:, 0:1], -float(cfg.SHIFT), writes=(b_ones,))
    p.call("V", "memset", c_qs[:, 0:1], float(ND + RD) ** -0.5, writes=(b_ones,))

    cast_list = []
    cast_list.append((xbf_d, xT, D, S, 512, "s_cx"))
    cast_list.append((wqa_bf, w_qa, D, QR, 1024, "s_cwqa"))
    cast_list.append((wkva_bf, w_kva, D, KVR + RD, 2048, "s_cwkva"))
    cast_list.append((wqb_bf.rearrange("h p k d -> (h p) (k d)"),
                      w_qb_r.rearrange("h p k d -> (h p) (k d)"),
                      NH * 128, QKc * QKD, 512, "s_cwqb"))
    cast_list.append((wkvb_bf.rearrange("h p k d -> (h p) (k d)"),
                      w_kvb_r.rearrange("h p k d -> (h p) (k d)"),
                      NH * 128, KKc * HD, 1024, "s_cwkvb"))
    cast_totals = {}
    b_cast = {}
    for dst, src, rows, cols, rc, sem in cast_list:
        b_cast[sem] = Buf(sem)
        for r0 in range(0, rows, rc):
            r1 = min(rows, r0 + rc)
            p.dma("G", dst[r0:r1, :], src[r0:r1, :], sem, writes=(b_cast[sem],))
            cast_totals[sem] = cast_totals.get(sem, 0) + 16

    def cast_wait(q, sem):
        p.ops.append(Op(q, "wait", args=(("dma", sem, cast_totals[sem]),)))

    m_p1 = arena.mark()
    x_bf = arena.bf(DK * TG)
    qc_f32 = arena.f32(QKc * TG)
    kc_f32 = arena.f32(KKc * TG)
    kr_raw = arena.f32(TG)
    rot_buf = arena.f32(TG)
    wqa_slab = [arena.bf(512) for _ in range(6)]
    wkva_slab = [arena.bf(512) for _ in range(6)]
    wkr_slab = [arena.bf(64) for _ in range(4)]
    sq_sl = [arena.bf(TG) for _ in range(2)]
    rbc = arena.f32(TG)
    tmpf = arena.f32(TG)
    tmp64a = tmpf
    rows1 = Rows(smalls, TG)
    r_ms = rows1.row(); r_s0 = rows1.row(); r_r0 = rows1.row()
    r_u = rows1.row(); r_s1 = rows1.row(); r_d = rows1.row(); r_inv = rows1.row()

    b_x = Buf("x_bf")
    b_qcf = Buf("qc_f32")
    b_kcf = Buf("kc_f32")
    b_krr = Buf("kr_raw")
    b_rot = Buf("rot_buf")
    b_t64 = Buf("tmp64a")
    b_wqa = [Buf(f"wqa{i}") for i in range(6)]
    b_wkva = [Buf(f"wkva{i}") for i in range(6)]
    b_wkr = [Buf(f"wkr{i}") for i in range(4)]
    b_sq = [Buf(f"sq{i}") for i in range(2)]
    b_rbc = Buf("rbc")
    b_tmpf = Buf("tmpf")
    b_rows = Buf("rows1")

    def rms_chain(n_feat, eps):
        p.call("V", "tensor_scalar_mul", r_ms[:, 0:TG], ps[7][0:1, 0:TG], 1.0 / n_feat,
               reads=(psb[7],), writes=(b_rows,))
        p.call("A", "activation", r_s0[:, 0:TG], r_ms[:, 0:TG], AF.Sqrt,
               reads=(b_rows,), writes=(b_rows,))
        p.call("V", "reciprocal", r_r0[:, 0:TG], r_s0[:, 0:TG], reads=(b_rows,), writes=(b_rows,))
        p.call("V", "tensor_mul", r_u[:, 0:TG], r_ms[:, 0:TG], r_r0[:, 0:TG],
               reads=(b_rows,), writes=(b_rows,))
        p.call("V", "tensor_add", r_s1[:, 0:TG], r_u[:, 0:TG], r_s0[:, 0:TG],
               reads=(b_rows,), writes=(b_rows,))
        p.call("V", "tensor_scalar", r_d[:, 0:TG], r_s1[:, 0:TG], 0.5, eps,
               ALU.mult, ALU.add, reads=(b_rows,), writes=(b_rows,))
        p.call("V", "reciprocal", r_inv[:, 0:TG], r_d[:, 0:TG], reads=(b_rows,), writes=(b_rows,))
        p.call("T", "matmul", ps[6][:, 0:TG], onesrow_for(r_inv), r_inv[:, 0:TG],
               start=True, stop=True, reads=(b_rows, b_ones), writes=(psb[6],))
        p.call("V", "tensor_copy", rbc, ps[6][:, 0:TG], reads=(psb[6],), writes=(b_rbc,))

    cast_wait("S", "s_cx")
    cast_wait("S", "s_cwqa")
    cast_wait("S", "s_cwkva")

    def p1_load_x(tg):
        t0 = tg * TG
        for kc in range(DK):
            p.dma("S", x_bf[:, kc * TG:(kc + 1) * TG], xbf_d[kc * 128:(kc + 1) * 128, t0:t0 + TG],
                  "g_x", writes=(b_x,))

    p1_load_x(0)
    for tg in range(NT):
        t0 = tg * TG

        nMb = (QKc + 3) // 4
        for Mb in range(nMb):
            ms = Mb * 4
            nm = min(4, QKc - ms)
            par = (Mb % 2) * 4
            for kc in range(DK):
                sl = kc % 6
                p.dma("S", wqa_slab[sl][:, 0:nm * 128],
                      wqa_bf[kc * 128:(kc + 1) * 128, ms * 128:(ms + nm) * 128],
                      f"g_wqa{sl}", writes=(b_wqa[sl],))
                for m in range(nm):
                    p.call("T", "matmul", ps[par + m][:, 0:TG],
                           wqa_slab[sl][:, m * 128:(m + 1) * 128],
                           x_bf[:, kc * TG:(kc + 1) * TG],
                           start=(kc == 0), stop=(kc == DK - 1),
                           reads=(b_wqa[sl], b_x), writes=(psb[par + m],))
            for m in range(nm):
                p.call("V", "tensor_copy", qc_f32[:, (ms + m) * TG:(ms + m + 1) * TG],
                       ps[par + m][:, 0:TG], reads=(psb[par + m],), writes=(b_qcf,))

        qsq = arena
        for mq in range(QKc):
            sl = mq % 2
            p.call("A", "activation", sq_sl[sl], qc_f32[:, mq * TG:(mq + 1) * TG], AF.Square,
                   reads=(b_qcf,), writes=(b_sq[sl],))
            p.call("T", "matmul", ps[7][0:1, 0:TG], ones_bf[:, 0:1], sq_sl[sl],
                   start=(mq == 0), stop=(mq == QKc - 1),
                   reads=(b_sq[sl], b_ones), writes=(psb[7],))

        for kc in range(DK):
            sl = kc % 6
            p.dma("S", wkva_slab[sl][:, 0:KKc * 128],
                  wkva_bf[kc * 128:(kc + 1) * 128, 0:KVR], f"g_wkva{sl}", writes=(b_wkva[sl],))
            for m in range(KKc):
                p.call("T", "matmul", ps[m][:, 0:TG],
                       wkva_slab[sl][:, m * 128:(m + 1) * 128],
                       x_bf[:, kc * TG:(kc + 1) * TG],
                       start=(kc == 0), stop=(kc == DK - 1),
                       reads=(b_wkva[sl], b_x), writes=(psb[m],))
        for m in range(KKc):
            p.call("V", "tensor_copy", kc_f32[:, m * TG:(m + 1) * TG], ps[m][:, 0:TG],
                   reads=(psb[m],), writes=(b_kcf,))

        for kc in range(DK):
            sl = kc % 4
            p.dma("S", wkr_slab[sl], wkva_bf[kc * 128:(kc + 1) * 128, KVR:KVR + RD],
                  f"g_wkr{sl}", writes=(b_wkr[sl],))
            p.call("T", "matmul", ps[5][0:64, 0:TG], wkr_slab[sl][:, 0:64],
                   x_bf[:, kc * TG:(kc + 1) * TG],
                   start=(kc == 0), stop=(kc == DK - 1),
                   reads=(b_wkr[sl], b_x), writes=(psb[5],))

        if tg + 1 < NT:
            p1_load_x(tg + 1)

        rms_chain(QR, cfg.EPS_RMS)
        for mq in range(QKc):
            p.call("V", "tensor_mul", tmpf, qc_f32[:, mq * TG:(mq + 1) * TG], rbc,
                   reads=(b_qcf, b_rbc), writes=(b_tmpf,))
            p.call("V", "tensor_scalar_mul", qcn[:, mq * S + t0:mq * S + t0 + TG],
                   tmpf, g_qa_sb[:, mq:mq + 1],
                   reads=(b_tmpf, b_gqa), writes=(b_qcn,))

        for mq in range(KKc):
            sl = mq % 2
            p.call("A", "activation", sq_sl[sl], kc_f32[:, mq * TG:(mq + 1) * TG], AF.Square,
                   reads=(b_kcf,), writes=(b_sq[sl],))
            p.call("T", "matmul", ps[7][0:1, 0:TG], ones_bf[:, 0:1], sq_sl[sl],
                   start=(mq == 0), stop=(mq == KKc - 1),
                   reads=(b_sq[sl], b_ones), writes=(psb[7],))
        rms_chain(KVR, cfg.EPS_RMS)
        for mq in range(KKc):
            p.call("V", "tensor_mul", tmpf, kc_f32[:, mq * TG:(mq + 1) * TG], rbc,
                   reads=(b_kcf, b_rbc), writes=(b_tmpf,))
            p.call("V", "tensor_scalar_mul", kcn[:, mq * S + t0:mq * S + t0 + TG],
                   tmpf, g_kva_sb[:, mq:mq + 1],
                   reads=(b_tmpf, b_gkva), writes=(b_kcn,))

        p.call("V", "tensor_copy", kr_raw[0:64, :], ps[5][0:64, 0:TG],
               reads=(psb[5],), writes=(b_krr,))
        p.dma("S", rot_buf[0:32, :], kr_raw[32:64, :], "s_rot", reads=(b_krr,), writes=(b_rot,))
        p.dma("S", rot_buf[32:64, :], kr_raw[0:32, :], "s_rot", reads=(b_krr,), writes=(b_rot,))
        p.call("V", "tensor_mul", tmp64a[0:64, :], kr_raw[0:64, :], t_cos[0:64, t0:t0 + TG],
               reads=(b_krr, b_tabs), writes=(b_t64, b_tmpf))
        p.call("V", "tensor_mul", rot_buf[0:64, :], rot_buf[0:64, :], t_sinr[0:64, t0:t0 + TG],
               reads=(b_rot, b_tabs), writes=(b_rot,))
        p.call("V", "tensor_add", krope[0:64, t0:t0 + TG], tmp64a[0:64, :], rot_buf[0:64, :],
               reads=(b_t64, b_rot), writes=(b_krope,))

    p.barrier()
    arena.reset(m_p1)

    big_casts = [(wo_bf, w_o, OD, D, 512, "s_cwo"),
                 (gate_bf, gateT, D, INTER, 256, "s_cg"),
                 (up_bf, upT, D, INTER, 256, "s_cu"),
                 (down_bf, downT, INTER, D, 1024, "s_cd")]
    for dst, src, rows, cols, rc, sem in big_casts:
        b_cast[sem] = Buf(sem)
        for r0 in range(0, rows, rc):
            r1 = min(rows, r0 + rc)
            p.dma("G", dst[r0:r1, :], src[r0:r1, :], sem, writes=(b_cast[sem],))
            cast_totals[sem] = cast_totals.get(sem, 0) + 16

    k_h = [arena.bf(S) for _ in range(2)]
    v_h = [arena.bf(S) for _ in range(2)]
    qn_h = [arena.bf(S) for _ in range(2)]
    qr_h = [arena.bf(S) for _ in range(2)]
    P_T = arena.bf(TCH * QG)
    wqb_sl = [arena.bf(QKc * QKD) for _ in range(2)]
    wkvb_sl = [arena.bf(KKc * HD) for _ in range(2)]
    qr_raw = arena.f32(QG)
    qrot = arena.f32(QG)
    qtmp = arena.f32(QG)
    rbc2 = arena.f32(QG)
    o_ev = [arena.bf(QG) for _ in range(2)]
    rows2 = Rows(smalls, TG)
    r2_r = rows2.row()

    b_kh = [Buf(f"k_h{i}") for i in range(2)]
    b_vh = [Buf(f"v_h{i}") for i in range(2)]
    b_qnh = [Buf(f"qn_h{i}") for i in range(2)]
    b_qrh = [Buf(f"qr_h{i}") for i in range(2)]
    b_PT = Buf("P_T")
    b_wqb = [Buf(f"wqb{i}") for i in range(2)]
    b_wkvb = [Buf(f"wkvb{i}") for i in range(2)]
    b_qrr = Buf("qr_raw"); b_qrot = Buf("qrot"); b_qtmp = Buf("qtmp")
    b_rbc2 = Buf("rbc2")
    b_oev = [Buf(f"o_ev{i}") for i in range(2)]
    b_rows2 = Buf("rows2")
    b_odram = Buf("o_dram")

    cast_wait("S", "s_cwqb")
    cast_wait("S", "s_cwkvb")

    def kvq_gen(h):
        sl = h % 2
        pp = [5, 6]
        pc = [0]

        def bank():
            b = pp[pc[0] % 2]
            pc[0] += 1
            return b

        p.dma("S", wqb_sl[sl], wqb_bf[h].rearrange("p k d -> p (k d)"),
              f"g_wqb{sl}", writes=(b_wqb[sl],))
        p.dma("S", wkvb_sl[sl], wkvb_bf[h].rearrange("p k d -> p (k d)"),
              f"g_wkvb{sl}", writes=(b_wkvb[sl],))
        yield
        for tg in range(NT):
            t0 = tg * TG
            bk = bank()
            for m in range(KKc):
                p.call("T", "matmul", ps[bk][:, 0:TG],
                       wkvb_sl[sl][:, m * HD:m * HD + 128],
                       kcn[:, m * S + t0:m * S + t0 + TG],
                       start=(m == 0), stop=(m == KKc - 1),
                       reads=(b_wkvb[sl], b_kcn), writes=(psb[bk],))
            yield
            p.call("V", "tensor_copy", k_h[sl][:, t0:t0 + TG], ps[bk][:, 0:TG],
                   reads=(psb[bk],), writes=(b_kh[sl],))
            yield
        for tq in range(TCH // 4):
            bk = bank()
            for u in range(4):
                tc = tq * 4 + u
                for m in range(KKc):
                    p.call("T", "matmul", ps[bk][:, u * VD:(u + 1) * VD],
                           kcn[:, m * S + tc * 128:m * S + (tc + 1) * 128],
                           wkvb_sl[sl][:, m * HD + 128:(m + 1) * HD],
                           start=(m == 0), stop=(m == KKc - 1),
                           reads=(b_wkvb[sl], b_kcn), writes=(psb[bk],))
                yield
            p.call("V", "tensor_copy", v_h[sl][:, tq * 512:(tq + 1) * 512], ps[bk][:, 0:512],
                   reads=(psb[bk],), writes=(b_vh[sl],))
            yield
        for tg in range(NT):
            t0 = tg * TG
            bk = bank()
            for m in range(QKc):
                p.call("T", "matmul", ps[bk][:, 0:TG],
                       wqb_sl[sl][:, m * QKD:m * QKD + ND],
                       qcn[:, m * S + t0:m * S + t0 + TG],
                       start=(m == 0), stop=(m == QKc - 1),
                       reads=(b_wqb[sl], b_qcn), writes=(psb[bk],))
                if m % 4 == 3:
                    yield
            p.call("A", "activation", qn_h[sl][:, t0:t0 + TG], ps[bk][:, 0:TG], AF.Copy,
                   scale=c_qs[:, 0:1], reads=(psb[bk], b_ones), writes=(b_qnh[sl],))
            yield
            bk = bank()
            for m in range(QKc):
                p.call("T", "matmul", ps[bk][0:64, 0:TG],
                       wqb_sl[sl][:, m * QKD + ND:(m + 1) * QKD],
                       qcn[:, m * S + t0:m * S + t0 + TG],
                       start=(m == 0), stop=(m == QKc - 1),
                       reads=(b_wqb[sl], b_qcn), writes=(psb[bk],))
                if m % 4 == 3:
                    yield
            p.call("V", "tensor_copy", qr_raw[0:64, :], ps[bk][0:64, 0:TG],
                   reads=(psb[bk],), writes=(b_qrr,))
            p.dma("S", qrot[0:32, :], qr_raw[32:64, :], "s_rot", reads=(b_qrr,), writes=(b_qrot,))
            p.dma("S", qrot[32:64, :], qr_raw[0:32, :], "s_rot", reads=(b_qrr,), writes=(b_qrot,))
            yield
            p.call("V", "tensor_mul", qtmp[0:64, :], qr_raw[0:64, :], t_cosq[0:64, t0:t0 + TG],
                   reads=(b_qrr, b_tabs), writes=(b_qtmp,))
            p.call("V", "tensor_mul", qrot[0:64, :], qrot[0:64, :], t_sinrq[0:64, t0:t0 + TG],
                   reads=(b_qrot, b_tabs), writes=(b_qrot,))
            p.call("V", "tensor_add", qr_h[sl][0:64, t0:t0 + TG], qtmp[0:64, :], qrot[0:64, :],
                   reads=(b_qtmp, b_qrot), writes=(b_qrh[sl],))
            yield

    def drain(gen, n=-1):
        if gen is None:
            return
        try:
            if n < 0:
                while True:
                    next(gen)
            else:
                for _ in range(n):
                    next(gen)
        except StopIteration:
            pass

    drain(kvq_gen(0))
    for h in range(NH):
        sl = h % 2
        gen = kvq_gen(h + 1) if h + 1 < NH else None
        for qg in range(NQ):
            q0 = qg * QG
            for kc in range(TCH):
                bk = 2 + kc % 3
                p.call("T", "matmul", ps[bk][:, 0:QG],
                       k_h[sl][:, kc * 128:(kc + 1) * 128], qn_h[sl][:, q0:q0 + QG],
                       start=True, stop=False, reads=(b_kh[sl], b_qnh[sl]), writes=(psb[bk],))
                p.call("T", "matmul", ps[bk][:, 0:QG],
                       krope[0:64, kc * 128:(kc + 1) * 128], qr_h[sl][0:64, q0:q0 + QG],
                       start=False, stop=True, reads=(b_krope, b_qrh[sl]), writes=(psb[bk],))
                p.call("A", "activation", P_T[:, kc * QG:(kc + 1) * QG], ps[bk][:, 0:QG],
                       AF.Exp, bias=c_shift[:, 0:1], reads=(psb[bk], b_ones), writes=(b_PT,))
                if kc % 4 == 3:
                    drain(gen, 2)
            for kc in range(TCH):
                p.call("T", "matmul", ps[7][0:1, 0:QG], ones_bf[:, 0:1],
                       P_T[:, kc * QG:(kc + 1) * QG],
                       start=(kc == 0), stop=(kc == TCH - 1),
                       reads=(b_PT, b_ones), writes=(psb[7],))
            p.call("V", "reciprocal", r2_r[:, 0:QG], ps[7][0:1, 0:QG],
                   reads=(psb[7],), writes=(b_rows2,))
            p.call("T", "matmul", ps[1][:, 0:QG], onesrow_for(r2_r), r2_r[:, 0:QG],
                   start=True, stop=True, reads=(b_rows2, b_ones), writes=(psb[1],))
            p.call("V", "tensor_copy", rbc2, ps[1][:, 0:QG], reads=(psb[1],), writes=(b_rbc2,))
            for kc in range(TCH):
                p.call("T", "matmul", ps[0][:, 0:QG],
                       v_h[sl][:, kc * 128:(kc + 1) * 128], P_T[:, kc * QG:(kc + 1) * QG],
                       start=(kc == 0), stop=(kc == TCH - 1),
                       reads=(b_vh[sl], b_PT), writes=(psb[0],))
            oev = o_ev[qg % 2]
            p.call("V", "tensor_tensor", oev, ps[0][:, 0:QG], rbc2, op=ALU.mult,
                   reads=(psb[0], b_rbc2), writes=(b_oev[qg % 2],))
            p.dma("S", o_dram[h * 128:(h + 1) * 128, q0:q0 + QG], oev, f"s_o{qg % 2}",
                  reads=(b_oev[qg % 2],), writes=(b_odram,))
            drain(gen, 5)
        drain(gen)

    p.barrier()
    arena.reset(m_persist)


    h1nbf = arena.bf(DK * TG)
    b_h1nbf = Buf("h1nbf")
    m_tg = arena.mark()

    b_h1nd = Buf("h1n_dram")
    b_h2d = Buf("h2_dram")
    b_wobf = b_cast["s_cwo"]
    b_gbf = b_cast["s_cg"]
    b_ubf = b_cast["s_cu"]
    b_dbf = b_cast["s_cd"]

    def slab_load(tg, slab_ap, bf_src, f32_src, sem, wb_sem, b_slab, b_scr):
        p.dma("S", slab_ap, bf_src, sem, reads=(b_scr,), writes=(b_slab,))

    DMB = DK // 4
    IBS = []
    ic0 = 0
    while ic0 < IC:
        IBS.append((ic0, min(4, IC - ic0)))
        ic0 += 4

    for tg in range(NT):
        t0 = tg * TG

        o_tile = arena.bf(ODK * TG)
        wo_sl = [arena.bf(512) for _ in range(4)]
        h1_f32 = arena.f32(DK * TG)
        x_sl = [arena.f32(TG) for _ in range(2)]
        hb_sl = [arena.bf(TG) for _ in range(2)]
        sq3_sl = [arena.bf(TG) for _ in range(2)]
        h1n_sl = [arena.f32(TG) for _ in range(2)]
        abc = arena.f32(TG)
        bbc = arena.f32(TG)
        tmp3 = x_sl[0]
        rows3 = Rows(smalls, TG)
        r3_mean = rows3.row(); r3_msq = rows3.row()
        r3_var = rows3.row(); r3_s0 = rows3.row(); r3_r0 = rows3.row(); r3_a = rows3.row()
        r3_b = rows3.row(); r3_c = rows3.row(); r3_A = rows3.row(); r3_B = rows3.row()

        b_ot = Buf("o_tile")
        b_wo = [Buf(f"wo{i}") for i in range(4)]
        b_h1 = Buf("h1_f32")
        b_xs = [Buf(f"x{i}") for i in range(2)]
        b_hb = [Buf(f"hb{i}") for i in range(2)]
        b_sq3 = [Buf(f"sq3{i}") for i in range(2)]
        b_h1n = [Buf(f"h1n{i}") for i in range(2)]
        b_abc = Buf("abc"); b_bbc = Buf("bbc")
        b_tmp3 = Buf("tmp3")
        b_rows3 = Buf("rows3")

        for kc in range(ODK):
            p.dma("S", o_tile[:, kc * TG:(kc + 1) * TG],
                  o_dram[kc * 128:(kc + 1) * 128, t0:t0 + TG], "s_ol",
                  reads=(b_odram,), writes=(b_ot,))
        cast_wait("S", "s_cwo")
        for mcb in range(DMB):
            par = (mcb % 2) * 4
            for m in range(2):
                mc = mcb * 4 + m
                p.dma("S", x_sl[m], xT[mc * 128:(mc + 1) * 128, t0:t0 + TG], f"s_x{m}",
                      writes=(b_xs[m],))
            for kc in range(ODK):
                wsl = kc % 4
                slab_load(tg, wo_sl[wsl],
                          wo_bf[kc * 128:(kc + 1) * 128, mcb * 512:(mcb + 1) * 512],
                          w_o[kc * 128:(kc + 1) * 128, mcb * 512:(mcb + 1) * 512],
                          f"s_wo{wsl}", "s_wbo", b_wo[wsl], b_wobf)
                for m in range(4):
                    p.call("T", "matmul", ps[par + m][:, 0:TG],
                           wo_sl[wsl][:, m * 128:(m + 1) * 128],
                           o_tile[:, kc * TG:(kc + 1) * TG],
                           start=(kc == 0), stop=(kc == ODK - 1),
                           reads=(b_wo[wsl], b_ot), writes=(psb[par + m],))
            for m in range(4):
                mc = mcb * 4 + m
                xs = m % 2
                if m >= 2:
                    p.dma("S", x_sl[xs], xT[mc * 128:(mc + 1) * 128, t0:t0 + TG], f"s_x{xs}",
                          writes=(b_xs[xs],))
                p.call("V", "tensor_add", h1_f32[:, mc * TG:(mc + 1) * TG],
                       ps[par + m][:, 0:TG], x_sl[xs],
                       reads=(psb[par + m], b_xs[xs]), writes=(b_h1,))

        for mc in range(DK):
            sl2 = mc % 2
            p.call("V", "tensor_copy", hb_sl[sl2], h1_f32[:, mc * TG:(mc + 1) * TG],
                   reads=(b_h1,), writes=(b_hb[sl2],))
            p.call("A", "activation", sq3_sl[sl2], h1_f32[:, mc * TG:(mc + 1) * TG], AF.Square,
                   reads=(b_h1,), writes=(b_sq3[sl2],))
            p.call("T", "matmul", ps[0][0:1, 0:TG], ones_bf[:, 0:1], hb_sl[sl2],
                   start=(mc == 0), stop=(mc == DK - 1),
                   reads=(b_hb[sl2], b_ones), writes=(psb[0],))
            p.call("T", "matmul", ps[1][0:1, 0:TG], ones_bf[:, 0:1], sq3_sl[sl2],
                   start=(mc == 0), stop=(mc == DK - 1),
                   reads=(b_sq3[sl2], b_ones), writes=(psb[1],))

        def ln_rows(rr, b_rows_, n):
            p.call("V", "tensor_scalar_mul", rr["mean"][:, 0:TG], ps[0][0:1, 0:TG], 1.0 / n,
                   reads=(psb[0],), writes=(b_rows_,))
            p.call("V", "tensor_scalar_mul", rr["msq"][:, 0:TG], ps[1][0:1, 0:TG], 1.0 / n,
                   reads=(psb[1],), writes=(b_rows_,))
            p.call("V", "tensor_mul", rr["a"][:, 0:TG], rr["mean"][:, 0:TG], rr["mean"][:, 0:TG],
                   reads=(b_rows_,), writes=(b_rows_,))
            p.call("V", "tensor_sub", rr["var"][:, 0:TG], rr["msq"][:, 0:TG], rr["a"][:, 0:TG],
                   reads=(b_rows_,), writes=(b_rows_,))
            p.call("V", "tensor_scalar_add", rr["var"][:, 0:TG], rr["var"][:, 0:TG], cfg.EPS_LN,
                   reads=(b_rows_,), writes=(b_rows_,))
            p.call("A", "activation", rr["s0"][:, 0:TG], rr["var"][:, 0:TG], AF.Sqrt,
                   reads=(b_rows_,), writes=(b_rows_,))
            p.call("V", "reciprocal", rr["r0"][:, 0:TG], rr["s0"][:, 0:TG],
                   reads=(b_rows_,), writes=(b_rows_,))
            p.call("V", "tensor_mul", rr["a"][:, 0:TG], rr["r0"][:, 0:TG], rr["r0"][:, 0:TG],
                   reads=(b_rows_,), writes=(b_rows_,))
            p.call("V", "tensor_mul", rr["b"][:, 0:TG], rr["a"][:, 0:TG], rr["var"][:, 0:TG],
                   reads=(b_rows_,), writes=(b_rows_,))
            p.call("V", "tensor_scalar", rr["c"][:, 0:TG], rr["b"][:, 0:TG], -0.5, 1.5,
                   ALU.mult, ALU.add, reads=(b_rows_,), writes=(b_rows_,))
            p.call("V", "tensor_mul", rr["A"][:, 0:TG], rr["c"][:, 0:TG], rr["r0"][:, 0:TG],
                   reads=(b_rows_,), writes=(b_rows_,))
            p.call("V", "tensor_mul", rr["b"][:, 0:TG], rr["mean"][:, 0:TG], rr["A"][:, 0:TG],
                   reads=(b_rows_,), writes=(b_rows_,))
            p.call("V", "tensor_scalar_mul", rr["B"][:, 0:TG], rr["b"][:, 0:TG], -1.0,
                   reads=(b_rows_,), writes=(b_rows_,))

        rr3 = {"mean": r3_mean, "msq": r3_msq, "var": r3_var, "s0": r3_s0, "r0": r3_r0,
               "a": r3_a, "b": r3_b, "c": r3_c, "A": r3_A, "B": r3_B}
        ln_rows(rr3, b_rows3, D)
        p.call("T", "matmul", ps[2][:, 0:TG], onesrow_for(r3_A), r3_A[:, 0:TG],
               start=True, stop=True, reads=(b_rows3, b_ones), writes=(psb[2],))
        p.call("T", "matmul", ps[3][:, 0:TG], onesrow_for(r3_B), r3_B[:, 0:TG],
               start=True, stop=True, reads=(b_rows3, b_ones), writes=(psb[3],))
        p.call("V", "tensor_copy", abc, ps[2][:, 0:TG], reads=(psb[2],), writes=(b_abc,))
        p.call("V", "tensor_copy", bbc, ps[3][:, 0:TG], reads=(psb[3],), writes=(b_bbc,))
        for mc in range(DK):
            sl2 = mc % 2
            p.call("V", "tensor_mul", tmp3, h1_f32[:, mc * TG:(mc + 1) * TG], abc,
                   reads=(b_h1, b_abc), writes=(b_tmp3, b_xs[0]))
            p.call("V", "tensor_add", h1n_sl[sl2], tmp3, bbc,
                   reads=(b_tmp3, b_bbc), writes=(b_h1n[sl2],))
            p.dma("S", h1n_dram[mc * 128:(mc + 1) * 128, t0:t0 + TG], h1n_sl[sl2], f"s_h1n{sl2}",
                  reads=(b_h1n[sl2],), writes=(b_h1nd,))
            p.call("A", "activation", h1nbf[:, mc * TG:(mc + 1) * TG], h1n_sl[sl2], AF.Copy,
                   reads=(b_h1n[sl2],), writes=(b_h1nbf,))

        p.barrier()
        arena.reset(m_tg)

        hmid = arena.bf(IC * TG)
        g_sl = [arena.bf(512) for _ in range(4)]
        u_sl = [arena.bf(512) for _ in range(4)]
        silu_sl = [arena.f32(TG) for _ in range(2)]
        b_hm = Buf("hmid")
        b_gsl = [Buf(f"g{i}") for i in range(4)]
        b_usl = [Buf(f"u{i}") for i in range(4)]
        b_silu = [Buf(f"silu{i}") for i in range(2)]

        cast_wait("S", "s_cg")
        cast_wait("S", "s_cu")
        for (i0, nic) in IBS:
            for kc in range(DK):
                wsl = kc % 4
                slab_load(tg, g_sl[wsl][:, 0:nic * 128],
                          gate_bf[kc * 128:(kc + 1) * 128, i0 * 128:(i0 + nic) * 128],
                          gateT[kc * 128:(kc + 1) * 128, i0 * 128:(i0 + nic) * 128],
                          f"s_g{wsl}", "s_wbg", b_gsl[wsl], b_gbf)
                slab_load(tg, u_sl[wsl][:, 0:nic * 128],
                          up_bf[kc * 128:(kc + 1) * 128, i0 * 128:(i0 + nic) * 128],
                          upT[kc * 128:(kc + 1) * 128, i0 * 128:(i0 + nic) * 128],
                          f"s_u{wsl}", "s_wbu", b_usl[wsl], b_ubf)
                for j in range(nic):
                    p.call("T", "matmul", ps[j][:, 0:TG],
                           g_sl[wsl][:, j * 128:(j + 1) * 128],
                           h1nbf[:, kc * TG:(kc + 1) * TG],
                           start=(kc == 0), stop=(kc == DK - 1),
                           reads=(b_gsl[wsl], b_h1nbf), writes=(psb[j],))
                    p.call("T", "matmul", ps[4 + j][:, 0:TG],
                           u_sl[wsl][:, j * 128:(j + 1) * 128],
                           h1nbf[:, kc * TG:(kc + 1) * TG],
                           start=(kc == 0), stop=(kc == DK - 1),
                           reads=(b_usl[wsl], b_h1nbf), writes=(psb[4 + j],))
            for j in range(nic):
                sl2 = j % 2
                p.call("A", "activation", silu_sl[sl2], ps[j][:, 0:TG], AF.Silu,
                       reads=(psb[j],), writes=(b_silu[sl2],))
                p.call("V", "tensor_tensor", hmid[:, (i0 + j) * TG:(i0 + j + 1) * TG],
                       silu_sl[sl2], ps[4 + j][:, 0:TG], op=ALU.mult,
                       reads=(b_silu[sl2], psb[4 + j]), writes=(b_hm,))

        p.barrier()
        arena.reset(m_tg)
        arena.bf(IC * TG)

        d_sl = [arena.bf(512) for _ in range(4)]
        h1r_sl = [arena.f32(TG) for _ in range(2)]
        h2_sl = [arena.f32(TG) for _ in range(2)]
        b_dsl = [Buf(f"d{i}") for i in range(4)]
        b_h1r = [Buf(f"h1r{i}") for i in range(2)]
        b_h2 = [Buf(f"h2{i}") for i in range(2)]

        cast_wait("S", "s_cd")
        for mcb in range(DMB):
            par = (mcb % 2) * 4
            for m in range(2):
                mc = mcb * 4 + m
                p.dma("S", h1r_sl[m], h1n_dram[mc * 128:(mc + 1) * 128, t0:t0 + TG],
                      f"s_h1r{m}", reads=(b_h1nd,), writes=(b_h1r[m],))
            for ic in range(IC):
                wsl = ic % 4
                slab_load(tg, d_sl[wsl],
                          down_bf[ic * 128:(ic + 1) * 128, mcb * 512:(mcb + 1) * 512],
                          downT[ic * 128:(ic + 1) * 128, mcb * 512:(mcb + 1) * 512],
                          f"s_d{wsl}", "s_wbd", b_dsl[wsl], b_dbf)
                for m in range(4):
                    p.call("T", "matmul", ps[par + m][:, 0:TG],
                           d_sl[wsl][:, m * 128:(m + 1) * 128],
                           hmid[:, ic * TG:(ic + 1) * TG],
                           start=(ic == 0), stop=(ic == IC - 1),
                           reads=(b_dsl[wsl], b_hm), writes=(psb[par + m],))
            for m in range(4):
                mc = mcb * 4 + m
                sl2 = mc % 2
                if m >= 2:
                    p.dma("S", h1r_sl[sl2], h1n_dram[mc * 128:(mc + 1) * 128, t0:t0 + TG],
                          f"s_h1r{sl2}", reads=(b_h1nd,), writes=(b_h1r[sl2],))
                p.call("V", "tensor_add", h2_sl[sl2], ps[par + m][:, 0:TG], h1r_sl[sl2],
                       reads=(psb[par + m], b_h1r[sl2]), writes=(b_h2[sl2],))
                p.dma("S", h2_dram[mc * 128:(mc + 1) * 128, t0:t0 + TG], h2_sl[sl2],
                      f"s_h2{sl2}", reads=(b_h2[sl2],), writes=(b_h2d,))

        h2r_sl = [arena.f32(TG) for _ in range(2)]
        out_sl = [arena.f32(TG) for _ in range(2)]
        hb2_sl = [arena.bf(TG) for _ in range(2)]
        sq2_sl = [arena.bf(TG) for _ in range(2)]
        abc2 = arena.f32(TG)
        bbc2 = arena.f32(TG)
        tmp4 = h1r_sl[0]
        b_h2r = [Buf(f"h2r{i}") for i in range(2)]
        b_out = [Buf(f"out{i}") for i in range(2)]
        b_hb2 = [Buf(f"hb2{i}") for i in range(2)]
        b_sq2 = [Buf(f"sq2{i}") for i in range(2)]
        b_abc2 = Buf("abc2"); b_bbc2 = Buf("bbc2")
        b_tmp4 = Buf("tmp4")
        rows4 = Rows(smalls, TG)
        rr4 = {k: rows4.row() for k in ("mean", "msq", "var", "s0", "r0", "a", "b", "c", "A", "B")}
        b_rows4 = Buf("rows4")

        for sem in ("s_h20", "s_h21"):
            if sem in p.dma_cnt:
                p.ops.append(Op("S", "wait", args=(("dma", sem, p.dma_cnt[sem]),)))
        for mc in range(DK):
            sl2 = mc % 2
            p.dma("S", h2r_sl[sl2], h2_dram[mc * 128:(mc + 1) * 128, t0:t0 + TG],
                  f"s_h2r{sl2}", reads=(b_h2d,), writes=(b_h2r[sl2],))
            p.call("V", "tensor_copy", hb2_sl[sl2], h2r_sl[sl2],
                   reads=(b_h2r[sl2],), writes=(b_hb2[sl2],))
            p.call("A", "activation", sq2_sl[sl2], h2r_sl[sl2], AF.Square,
                   reads=(b_h2r[sl2],), writes=(b_sq2[sl2],))
            p.call("T", "matmul", ps[0][0:1, 0:TG], ones_bf[:, 0:1], hb2_sl[sl2],
                   start=(mc == 0), stop=(mc == DK - 1),
                   reads=(b_hb2[sl2], b_ones), writes=(psb[0],))
            p.call("T", "matmul", ps[1][0:1, 0:TG], ones_bf[:, 0:1], sq2_sl[sl2],
                   start=(mc == 0), stop=(mc == DK - 1),
                   reads=(b_sq2[sl2], b_ones), writes=(psb[1],))
        ln_rows(rr4, b_rows4, D)
        p.call("T", "matmul", ps[2][:, 0:TG], onesrow_for(rr4["A"]), rr4["A"][:, 0:TG],
               start=True, stop=True, reads=(b_rows4, b_ones), writes=(psb[2],))
        p.call("T", "matmul", ps[3][:, 0:TG], onesrow_for(rr4["B"]), rr4["B"][:, 0:TG],
               start=True, stop=True, reads=(b_rows4, b_ones), writes=(psb[3],))
        p.call("V", "tensor_copy", abc2, ps[2][:, 0:TG], reads=(psb[2],), writes=(b_abc2,))
        p.call("V", "tensor_copy", bbc2, ps[3][:, 0:TG], reads=(psb[3],), writes=(b_bbc2,))
        for mc in range(DK):
            sl2 = mc % 2
            p.dma("S", h2r_sl[sl2], h2_dram[mc * 128:(mc + 1) * 128, t0:t0 + TG],
                  f"s_h2r{sl2}", reads=(b_h2d,), writes=(b_h2r[sl2],))
            p.call("V", "tensor_mul", tmp4, h2r_sl[sl2], abc2,
                   reads=(b_h2r[sl2], b_abc2), writes=(b_tmp4, b_h1r[0]))
            p.call("V", "tensor_add", out_sl[sl2], tmp4, bbc2,
                   reads=(b_tmp4, b_bbc2), writes=(b_out[sl2],))
            p.dma("S", outT[mc * 128:(mc + 1) * 128, t0:t0 + TG], out_sl[sl2], f"s_out{sl2}",
                  reads=(b_out[sl2],))

        p.barrier()
        arena.reset(m_tg)

    p.finish(["s_out0", "s_out1"])
    p.number()

    sem_names = sorted(set([op.sem for op in p.ops if op.kind == "dma"]
                           + [op.args[0][0] for op in p.ops if op.kind == "wait"]
                           + ["ENG_" + op.eng for op in p.ops
                              if op.kind == "call" and op.demand]))
    sem_objs = {name: nc.alloc_semaphore(name) for name in sem_names}

    eng_sem = {e: sem_objs.get("ENG_" + e) for e in ("T", "A", "V", "S", "G")}

    def emit(eng_key, eng):
        for op in p.ops:
            if op.eng != eng_key:
                continue
            if op.kind == "wait":
                sem, val = op.args[0]
                eng.wait_ge(sem_objs[sem], val)
            elif op.kind == "dma":
                instr = eng.dma_start(out=op.args[0], in_=op.args[1], **op.kwargs)
                instr.then_inc(sem_objs[op.sem], 16)
            else:
                instr = getattr(eng, op.fn)(*op.args, **op.kwargs)
                if op.demand:
                    instr.then_inc(eng_sem[op.eng], 1)

    with nc.Block() as block:
        @block.tensor
        def _(t):
            emit("T", t)

        @block.scalar
        def _(a):
            emit("A", a)

        @block.vector
        def _(v):
            emit("V", v)

        @block.sync
        def _(s):
            emit("S", s)

        @block.gpsimd
        def _(g):
            emit("G", g)

    nc.finalize()
    return nc


def host_inputs(cfg, x_b, w_qa, g_qa, w_qb, w_kva, g_kva, w_kvb, w_o, w_gate, w_up, w_down):
    S, RD, NH, QKc, KKc = cfg.S, cfg.RD, cfg.NH, cfg.QR // 128, cfg.KVR // 128
    QKD = cfg.ND + cfg.RD
    HD = cfg.ND + cfg.VD
    scl = float(QKD) ** -0.5

    inv_freq = (1.0 / (10000.0 ** (np.arange(0, RD, 2, dtype=np.float32) / RD))).astype(np.float32)
    t = np.arange(S, dtype=np.float32)
    freqs = np.outer(t, inv_freq).astype(np.float32)
    emb = np.concatenate([freqs, freqs], axis=1)
    cos = np.cos(emb).T.astype(np.float32)
    sin = np.sin(emb).T.astype(np.float32)
    sin_rot = np.concatenate([-sin[:RD // 2], sin[RD // 2:]], axis=0).astype(np.float32)
    tabs = np.stack([cos, sin_rot, (cos * scl), (sin_rot * scl)]).astype(np.float32)

    w_qb_r = np.ascontiguousarray(
        w_qb.reshape(QKc, 128, NH, QKD).transpose(2, 1, 0, 3)).astype(np.float32)
    w_kvb_r = np.ascontiguousarray(
        w_kvb.reshape(KKc, 128, NH, HD).transpose(2, 1, 0, 3)).astype(np.float32)
    g_qa_r = np.ascontiguousarray(g_qa.reshape(QKc, 128).T).astype(np.float32)
    g_kva_r = np.ascontiguousarray(g_kva.reshape(KKc, 128).T).astype(np.float32)

    return {
        "xT": np.ascontiguousarray(x_b.T).astype(np.float32),
        "w_qa": np.ascontiguousarray(w_qa).astype(np.float32),
        "w_qb_r": w_qb_r,
        "w_kva": np.ascontiguousarray(w_kva).astype(np.float32),
        "w_kvb_r": w_kvb_r,
        "w_o": np.ascontiguousarray(w_o).astype(np.float32),
        "gateT": np.ascontiguousarray(w_gate.T).astype(np.float32),
        "upT": np.ascontiguousarray(w_up.T).astype(np.float32),
        "downT": np.ascontiguousarray(w_down.T).astype(np.float32),
        "tabs": tabs,
        "g_qa_r": g_qa_r,
        "g_kva_r": g_kva_r,
    }


def kernel(L0_attn_hidden_states, L0_attn_w_qa, L0_attn_g_qa, L0_attn_w_qb,
           L0_attn_w_kva, L0_attn_g_kva, L0_attn_w_kvb, L0_attn_w_o,
           L0_ffn_w_gate, L0_ffn_w_up, L0_ffn_w_down):
    cfg = CFG()
    x = np.asarray(L0_attn_hidden_states, dtype=np.float32)
    B = x.shape[0]
    assert B == 8 and x.shape[1] == cfg.S and x.shape[2] == cfg.D

    nc = build(cfg)

    shared = None
    in_maps = []
    for b in range(B):
        m = host_inputs(cfg, x[b],
                        np.asarray(L0_attn_w_qa), np.asarray(L0_attn_g_qa),
                        np.asarray(L0_attn_w_qb), np.asarray(L0_attn_w_kva),
                        np.asarray(L0_attn_g_kva), np.asarray(L0_attn_w_kvb),
                        np.asarray(L0_attn_w_o), np.asarray(L0_ffn_w_gate),
                        np.asarray(L0_ffn_w_up), np.asarray(L0_ffn_w_down))
        if shared is None:
            shared = {k: v for k, v in m.items() if k != "xT"}
        else:
            for k in shared:
                m[k] = shared[k]
        in_maps.append(m)

    res = run_bass_kernel_spmd(nc, in_maps, core_ids=list(range(8)))
    out = np.stack([np.asarray(r["outT"]).T for r in res.results], axis=0)
    return np.ascontiguousarray(out.astype(np.float32))
